# version 06
# baseline (speedup 1.0000x reference)
import math
from contextlib import ExitStack

import numpy as np

import concourse.bass as bass
import concourse.mybir as mybir
from concourse.bass import IndirectOffsetOnAxis
from concourse.bass_utils import run_bass_kernel_spmd

F32 = mybir.dt.float32
BF16 = mybir.dt.bfloat16
I32 = mybir.dt.int32
F32R = mybir.dt.float32r
OP = mybir.AluOpType
AF = mybir.ActivationFunctionType

NCORES = 8
T = 2048
NT = 16
D = 1024
DC = 8
FF = 4096
FC = 32
ROT = 512
NPAIR = 256
CAP = 1280
NG = CAP // 128
NBLK = 5
BIS = 56
BIGF = 262144.0
PI_F = 3.1415925
INV2PI = 1.0 / (2.0 * math.pi)
TWOPI = 2.0 * math.pi
LN1E4 = math.log(10000.0)
EPS_LN = 1e-5
EPS_L2 = 1e-6

_ENG = ("s", "g", "v", "a", "t")


class Plan:

    def __init__(self):
        self.items = {e: [] for e in _ENG}
        self.done = {e: 0 for e in _ENG}
        self.count = {}

    def op(self, eng, fn, deps=(), inc=1, sem=None):
        sem = sem or ("d_" + eng if inc == 16 else "c_" + eng)
        val = self.count.get(sem, 0) + inc
        self.count[sem] = val
        self.items[eng].append((fn, tuple(deps), sem, inc))
        return (sem, val)

    def emit_phase(self, nc, sems):
        pend = {e: self.items[e][self.done[e]:] for e in _ENG}
        if not any(pend.values()):
            return
        with nc.Block() as block:
            def make_body(entries, eng_name):
                hwm = {}
                def body(e):
                    for fn, deps, sem, inc in entries:
                        for (dsem, dval) in deps:
                            if hwm.get(dsem, -1) >= dval:
                                continue
                            assert dval <= self.count.get(dsem, 0), \
                                f"wait beyond final: {dsem} {dval}"
                            e.wait_ge(sems[dsem], dval)
                            hwm[dsem] = dval
                        inst = fn(e)
                        inst.then_inc(sems[sem], inc)
                        if sem in hwm or True:
                            hwm[sem] = max(hwm.get(sem, 0), 0)
                return body

            if pend["s"]:
                block.sync(make_body(pend["s"], "s"))
            if pend["g"]:
                block.gpsimd(make_body(pend["g"], "g"))
            if pend["v"]:
                block.vector(make_body(pend["v"], "v"))
            if pend["a"]:
                block.scalar(make_body(pend["a"], "a"))
            if pend["t"]:
                block.tensor(make_body(pend["t"], "t"))
        for e in _ENG:
            self.done[e] = len(self.items[e])


def build_nc(debug=False):
    nc = bass.Bass("TRN2", target_bir_lowering=False, debug=False)

    x_h = nc.dram_tensor("x", [T, D], F32, kind="ExternalInput").ap()
    winT_h = nc.dram_tensor("w_inT", [D, 3 * D], F32, kind="ExternalInput").ap()
    woutT_h = nc.dram_tensor("w_outT", [D, D], F32, kind="ExternalInput").ap()
    w1_h = nc.dram_tensor("w1", [D, FF], F32, kind="ExternalInput").ap()
    w2_h = nc.dram_tensor("w2", [FF, D], F32, kind="ExternalInput").ap()
    g_h = nc.dram_tensor("g_vec", [1, D], F32, kind="ExternalInput").ap()
    b_h = nc.dram_tensor("b_vec", [1, D], F32, kind="ExternalInput").ap()
    wr_h = nc.dram_tensor("wr_vec", [1, D], F32, kind="ExternalInput").ap()
    b2_h = nc.dram_tensor("b2_vec", [1, D], F32, kind="ExternalInput").ap()
    b1pm_h = nc.dram_tensor("b1_pm", [128, FC], F32, kind="ExternalInput").ap()
    pos_h = nc.dram_tensor("postile", [128, NT], F32, kind="ExternalInput").ap()
    ident_h = nc.dram_tensor("identity", [128, 128], BF16, kind="ExternalInput").ap()
    identf_h = nc.dram_tensor("identity_f", [128, 128], F32R, kind="ExternalInput").ap()
    onesr_h = nc.dram_tensor("ones_r", [128, 1], F32R, kind="ExternalInput").ap()
    l128_h = nc.dram_tensor("l128", [128, 128], F32, kind="ExternalInput").ap()

    y_h = nc.dram_tensor("y", [T, D], F32, kind="ExternalOutput").ap()
    if debug:
        dbg_stats = nc.dram_tensor("dbg_stats", [128, 8 * NT], F32, kind="ExternalOutput").ap()
        dbg_sc = nc.dram_tensor("dbg_sc", [128, NT], F32, kind="ExternalOutput").ap()
        dbg_bis = nc.dram_tensor("dbg_bis", [128, 8], F32, kind="ExternalOutput").ap()
        dbg_wid = nc.dram_tensor("dbg_wid", [128, 2 * NG], F32, kind="ExternalOutput").ap()
        dbg_aout = nc.dram_tensor("dbg_aout", [1, D], F32, kind="ExternalOutput").ap()
        dbg_h2 = nc.dram_tensor("dbg_h2", [T, D], mybir.dt.bfloat16, kind="ExternalOutput").ap()
        dbg_l2nq = nc.dram_tensor("dbg_l2nq", [128, NT * D], mybir.dt.bfloat16, kind="ExternalOutput").ap()
        dbg_mloc = nc.dram_tensor("dbg_mloc", [128, NT], F32, kind="ExternalOutput").ap()
        dbg_slot = nc.dram_tensor("dbg_slot", [128, NT], F32, kind="ExternalOutput").ap()
        dbg_sloti = nc.dram_tensor("dbg_sloti", [128, NT], I32, kind="ExternalOutput").ap()
        dbg_rc = nc.dram_tensor("dbg_rc", [128, 2], F32, kind="ExternalOutput").ap()
        dbg_widd = nc.dram_tensor("dbg_widd", [CAP, 2], F32, kind="ExternalOutput").ap()
        dbg_A2 = nc.dram_tensor("dbg_A2", [128, NT], F32, kind="ExternalOutput").ap()
        dbg_B2 = nc.dram_tensor("dbg_B2", [128, NT], F32, kind="ExternalOutput").ap()
        dbg_C2 = nc.dram_tensor("dbg_C2", [128, NT], F32, kind="ExternalOutput").ap()
        dbg_pairs = nc.dram_tensor("dbg_pairs", [128, 2 * NT], F32, kind="ExternalOutput").ap()
        dbg_gt = nc.dram_tensor("dbg_gt", [DC * 128, CAP], mybir.dt.bfloat16, kind="ExternalOutput").ap()

    h2tok = nc.dram_tensor("h2tok", [T, D], BF16).ap()
    widd = nc.dram_tensor("widd", [CAP, 2], F32).ap()
    a_in = nc.dram_tensor("a_in", [1, D], F32)
    a_out = nc.dram_tensor("a_out", [1, D], F32)
    sc_in = nc.dram_tensor("sc_in", [128, NT], F32)
    sc_out = nc.dram_tensor("sc_out", [4, T], F32)
    invf_d = nc.dram_tensor("invf_d", [1, NPAIR], F32).ap()

    groups = [[0, 1, 2, 3], [4, 5, 6, 7]]
    P = Plan()

    with ExitStack() as glob:
        eg = glob.enter_context
        sems = {n: eg(nc.semaphore(n)) for n in
                ("d_s", "d_g", "c_v", "c_a", "c_t", "c_g", "cc_a", "cc_sc",
                 "d_x0", "d_x1", "d_x2", "d_x3", "d_x4", "d_x5", "d_h2",
                 "d_ga0", "d_ga1", "d_wid",
                 "d_wq", "d_wk", "d_wv", "d_wo", "d_w1", "d_w2",
                 "d_b0", "d_b1", "d_sc0", "d_sc1", "d_const", "d_abc",
                 "d_invf", "d_sci", "d_ai", "d_scg", "d_wr", "d_b2")}

        l2nq = eg(nc.sbuf_tensor("l2nq", [128, NT * D], BF16))
        stats = eg(nc.sbuf_tensor("stats", [128, 8 * NT], F32))
        mu = lambda i: stats[:, i:i + 1]
        rr = lambda i: stats[:, NT + i:NT + i + 1]
        rho = lambda i: stats[:, 2 * NT + i:2 * NT + i + 1]
        mu2 = lambda i: stats[:, 3 * NT + i:3 * NT + i + 1]
        rr2 = lambda i: stats[:, 4 * NT + i:4 * NT + i + 1]
        rq_ = lambda i: stats[:, 5 * NT + i:5 * NT + i + 1]
        rk_ = lambda i: stats[:, 6 * NT + i:6 * NT + i + 1]
        tmps = eg(nc.sbuf_tensor("tmps", [128, 8], F32))
        scol = eg(nc.sbuf_tensor("scol", [128, 4], F32))
        consts = eg(nc.sbuf_tensor("consts", [128, 4], F32))
        scloc = eg(nc.sbuf_tensor("scloc", [128, NT], F32))
        onesf = eg(nc.sbuf_tensor("onesf", [128, 128], F32))
        onesbf = eg(nc.sbuf_tensor("onesbf", [128, 1], F32R))
        ident = eg(nc.sbuf_tensor("ident", [128, 128], BF16))
        identr = eg(nc.sbuf_tensor("identr", [128, 128], F32R))
        l128 = eg(nc.sbuf_tensor("l128t", [128, 128], F32))
        b1pm = eg(nc.sbuf_tensor("b1pm", [128, FC], F32))
        postile = eg(nc.sbuf_tensor("postile_sb", [128, NT], F32))
        idloc = eg(nc.sbuf_tensor("idloc", [128, NT], F32))
        actjunk = eg(nc.sbuf_tensor("actjunk", [128, D], F32))
        dvejunk = eg(nc.sbuf_tensor("dvejunk", [128, D], F32))
        asb = eg(nc.sbuf_tensor("asb", [1, D], F32))
        scg = eg(nc.sbuf_tensor("scg", [128, 64], F32))
        bis = eg(nc.sbuf_tensor("bis", [128, 8], F32))
        bmask = eg(nc.sbuf_tensor("bmask", [128, 64], F32))
        ccol = eg(nc.sbuf_tensor("ccol", [128, 1], F32))
        mloc = eg(nc.sbuf_tensor("mloc", [128, NT], F32))
        rowcnt = eg(nc.sbuf_tensor("rowcnt", [128, 1], F32))
        pbase = eg(nc.sbuf_tensor("pbase", [128, 1], F32))
        pfxa = eg(nc.sbuf_tensor("pfxa", [128, NT], F32))
        pfxb = eg(nc.sbuf_tensor("pfxb", [128, NT], F32))
        slot = eg(nc.sbuf_tensor("slot", [128, NT], F32))
        sloti = eg(nc.sbuf_tensor("sloti", [128, NT], I32))
        pairs = eg(nc.sbuf_tensor("pairs", [128, 2 * NT], F32))
        dbgA = eg(nc.sbuf_tensor("dbgA", [128, NT], F32))
        dbgB = eg(nc.sbuf_tensor("dbgB", [128, NT], F32))
        dbgC = eg(nc.sbuf_tensor("dbgC", [128, NT], F32))
        widpre = eg(nc.sbuf_tensor("widpre", [128, 2 * NG], F32))
        widpm = eg(nc.sbuf_tensor("widpm", [128, 2 * NG], F32))
        wslot = eg(nc.sbuf_tensor("wslot", [128, NG], F32))
        idsl = eg(nc.sbuf_tensor("idsl", [128, NG], I32))

        with ExitStack() as sA:
            ea = sA.enter_context
            xring = [ea(nc.sbuf_tensor(f"xr{k}", [128, D], F32)) for k in range(3)]
            winT = [ea(nc.sbuf_tensor(f"winT{q}", [128, DC * D], BF16)) for q in range(3)]
            sintab = ea(nc.sbuf_tensor("sintab", [128, NT * NPAIR], F32))
            costab = ea(nc.sbuf_tensor("costab", [128, NT * NPAIR], F32))
            tbl = ea(nc.sbuf_tensor("tbl", [128, 4 * NPAIR], F32))
            tbli = ea(nc.sbuf_tensor("tbli", [128, 4 * NPAIR], I32))
            theta = ea(nc.sbuf_tensor("theta", [128, 4 * NPAIR], F32))
            gbc = ea(nc.sbuf_tensor("gbc", [128, D], F32))
            bbc = ea(nc.sbuf_tensor("bbc", [128, D], F32))
            invfbc = ea(nc.sbuf_tensor("invfbc", [128, NPAIR], F32))
            qrot = ea(nc.sbuf_tensor("qrot", [128, D], F32))
            krot = ea(nc.sbuf_tensor("krot", [128, D], F32))
            kraw = ea(nc.sbuf_tensor("kraw", [128, D], F32))
            kvbf = ea(nc.sbuf_tensor("kvbf", [128, D], F32R))
            tq = ea(nc.sbuf_tensor("tq", [128, NPAIR], F32))
            tq2 = ea(nc.sbuf_tensor("tq2", [128, NPAIR], F32))
            tk1 = ea(nc.sbuf_tensor("tk1", [128, NPAIR], F32))
            tk2 = ea(nc.sbuf_tensor("tk2", [128, NPAIR], F32))
            xnbf = [ea(nc.sbuf_tensor(f"xnbf{k}", [128, D], BF16)) for k in range(2)]
            xnT = [ea(nc.sbuf_tensor(f"xnT{k}", [128, D], BF16)) for k in range(2)]
            psq = ea(nc.psum_tensor("psq", [128, D], F32))
            psk = ea(nc.psum_tensor("psk", [128, D], F32))
            psv = ea(nc.psum_tensor("psv", [128, D], F32))
            psA = ea(nc.psum_tensor("psA", [1, 512], F32))
            psT = ea(nc.psum_tensor("psT", [128, D], BF16))

            t_ident = P.op("s", lambda e: e.dma_start(out=ident[:], in_=ident_h[:, :]), inc=16, sem="d_const")
            t_identr = P.op("s", lambda e: e.dma_start(out=identr[:], in_=identf_h[:, :]), inc=16, sem="d_const")
            t_l128 = P.op("s", lambda e: e.dma_start(out=l128[:], in_=l128_h[:, :]), inc=16, sem="d_const")
            t_b1 = P.op("s", lambda e: e.dma_start(out=b1pm[:], in_=b1pm_h[:, :]), inc=16, sem="d_const")
            t_pos = P.op("s", lambda e: e.dma_start(out=postile[:], in_=pos_h[:, :]), inc=16, sem="d_const")
            t_gbc = P.op("s", lambda e: e.dma_start(
                out=gbc[:], in_=g_h.to_broadcast([128, D])), inc=16, sem="d_const")
            t_bbc = P.op("s", lambda e: e.dma_start(
                out=bbc[:], in_=b_h.to_broadcast([128, D])), inc=16, sem="d_const")

            t_onesf = P.op("v", lambda e: e.memset(onesf[:], 1.0))
            P.op("v", lambda e: e.memset(consts[:, 0:1], EPS_LN))
            P.op("v", lambda e: e.memset(consts[:, 1:2], EPS_L2))
            P.op("v", lambda e: e.memset(consts[:, 2:3], 1.0 + EPS_LN))
            t_consts = P.op("v", lambda e: e.memset(consts[:, 3:4], 4096.0))
            t_onesbf = P.op("s", lambda e: e.dma_start(out=onesbf[:], in_=onesr_h[:, :]), inc=16, sem="d_const")
            t_pre1 = P.op("v", lambda e: e.memset(widpre[:], BIGF))
            t_pre2 = P.op("v", lambda e: e.memset(widpre[:, 0:2 * NG:2], 0.0),
                          deps=(t_pre1,))
            t_pre_dma = P.op("g", lambda e: e.dma_start(
                out=widd.rearrange("(p j) two -> p (j two)", p=128),
                in_=widpre[:]), deps=(t_pre2,), inc=16, sem="d_wid")
            t_asb0 = P.op("v", lambda e: e.memset(asb[:], 0.0))
            t_idloc = P.op("g", lambda e: e.iota(
                idloc[:], pattern=[[128, NT]], base=0, channel_multiplier=1,
                allow_small_or_imprecise_dtypes=True))

            t_winT = []
            for q in range(3):
                for c in range(DC):
                    t_winT.append(P.op("g", (lambda q, c: (lambda e: e.dma_start(
                        out=winT[q][:, c * D:(c + 1) * D],
                        in_=winT_h[128 * c:128 * (c + 1), q * D:(q + 1) * D])))(q, c),
                        inc=16, sem=("d_wq", "d_wk", "d_wv")[q]))

            t_jrow = P.op("g", lambda e: e.iota(
                invfbc[0:1, :], pattern=[[1, NPAIR]], base=0, channel_multiplier=0,
                allow_small_or_imprecise_dtypes=True))
            t_invf = P.op("a", lambda e: e.activation(
                out=invfbc[0:1, :], in_=invfbc[0:1, :], func=AF.Exp,
                scale=-2.0 * LN1E4 / ROT), deps=(t_jrow,))
            t_invfst = P.op("s", lambda e: e.dma_start(
                out=invf_d[:, :], in_=invfbc[0:1, :]), deps=(t_invf,), inc=16, sem="d_invf")
            t_invfbc = P.op("s", lambda e: e.dma_start(
                out=invfbc[:], in_=invf_d.to_broadcast([128, NPAIR])),
                deps=(t_invfst,), inc=16, sem="d_invf")

            SEG = 4 * NPAIR
            t_sinseg = [None] * 4
            t_cosseg = [None] * 4

            def plan_tabseg(sgi):
                pos_b = postile[:, 4 * sgi:4 * (sgi + 1)] \
                    .rearrange("p (i o) -> p i o", o=1).to_broadcast([128, 4, NPAIR])
                invf_b = invfbc[:].rearrange("p (o j) -> p o j", o=1) \
                    .to_broadcast([128, 4, NPAIR])
                th3 = theta[:].rearrange("p (i j) -> p i j", i=4)
                dep_th = [t_pos, t_invfbc]
                if sgi > 0:
                    dep_th.append(t_cosseg[sgi - 1])
                t_theta = P.op("v", (lambda pos_b, invf_b, th3: (lambda e:
                    e.tensor_tensor(out=th3, in0=pos_b, in1=invf_b, op=OP.mult)))(
                        pos_b, invf_b, th3), deps=tuple(dep_th))

                def plan_trig(dst, shift, sgi, t_theta, extra_dep=None):
                    dep1 = [t_theta] + ([extra_dep] if extra_dep else [])
                    t1 = P.op("v", (lambda shift: (lambda e: e.tensor_scalar(
                        out=tbl[:], in0=theta[:], scalar1=INV2PI,
                        scalar2=shift * INV2PI, op0=OP.mult, op1=OP.add)))(shift),
                        deps=tuple(dep1))
                    t2 = P.op("v", lambda e: e.tensor_copy(out=tbli[:], in_=tbl[:]),
                              deps=(t1,))
                    t3 = P.op("v", lambda e: e.tensor_copy(out=tbl[:], in_=tbli[:]),
                              deps=(t2,))
                    t4 = P.op("v", lambda e: e.scalar_tensor_tensor(
                        out=tbl[:], in0=tbl[:], scalar=-TWOPI, in1=theta[:],
                        op0=OP.mult, op1=OP.add), deps=(t3,))
                    if shift != 0.0:
                        t5 = P.op("v", (lambda shift: (lambda e: e.tensor_scalar(
                            out=tbl[:], in0=tbl[:], scalar1=shift, scalar2=PI_F,
                            op0=OP.add, op1=OP.min)))(shift), deps=(t4,))
                        t6 = P.op("v", lambda e: e.tensor_scalar_max(
                            tbl[:], tbl[:], -PI_F), deps=(t5,))
                    else:
                        t6 = P.op("v", lambda e: e.tensor_scalar(
                            out=tbl[:], in0=tbl[:], scalar1=PI_F, scalar2=-PI_F,
                            op0=OP.min, op1=OP.max), deps=(t4,))
                    return P.op("a", (lambda dst, sgi: (lambda e: e.activation(
                        out=dst[:, sgi * SEG:(sgi + 1) * SEG], in_=tbl[:],
                        func=AF.Sin)))(dst, sgi), deps=(t6,))

                t_sinseg[sgi] = plan_trig(sintab, 0.0, sgi, t_theta)
                t_cosseg[sgi] = plan_trig(costab, math.pi / 2.0, sgi, t_theta,
                                          extra_dep=t_sinseg[sgi])

            t_xload = [None] * (NT + 2)
            t_stats = [None] * NT
            t_xn = [None] * NT
            t_slab = [None] * NT
            t_qmm = [None] * NT
            t_kmm = [None] * NT
            t_vmm = [None] * NT
            t_qrot = [None] * NT
            t_sqQ = [None] * NT
            t_l2nq = [None] * NT
            t_kraw = [None] * NT
            t_krot = [None] * NT
            t_sqK = [None] * NT
            t_kv = [None] * NT
            t_amm = [None] * NT

            def plan_stats(i, dep0):
                xr = xring[i % 3]
                tk = P.op("a", (lambda xr: (lambda e: e.activation(
                    out=actjunk[:], in_=xr[:], func=AF.Copy,
                    accum_out=scol[:, 0:1])))(xr), deps=dep0)
                tk2 = P.op("a", (lambda xr: (lambda e: e.activation(
                    out=actjunk[:], in_=xr[:], func=AF.Square,
                    accum_out=scol[:, 1:2])))(xr), deps=(tk,))
                tk = P.op("a", (lambda i: (lambda e: e.mul(
                    mu(i), scol[:, 0:1], 1.0 / D)))(i), deps=(tk,))
                tk = P.op("a", (lambda i: (lambda e: e.activation(
                    out=tmps[:, 0:1], in_=mu(i), func=AF.Square)))(i), deps=(tk,))
                tk = P.op("a", lambda e: e.activation(
                    out=tmps[:, 0:1], in_=tmps[:, 0:1], func=AF.Copy,
                    scale=-1.0, bias=EPS_LN), deps=(tk,))
                tk = P.op("a", lambda e: e.activation(
                    out=tmps[:, 1:2], in_=scol[:, 1:2], func=AF.Ln,
                    scale=1.0 / D, bias=tmps[:, 0:1]), deps=(tk, tk2))
                tkr = P.op("a", (lambda i: (lambda e: e.activation(
                    out=rr(i), in_=tmps[:, 1:2], func=AF.Exp, scale=-0.5)))(i),
                    deps=(tk,))
                tk = P.op("a", lambda e: e.activation(
                    out=tmps[:, 2:3], in_=tmps[:, 1:2], func=AF.Exp,
                    scale=-1.0), deps=(tk,))
                tk = P.op("a", lambda e: e.activation(
                    out=tmps[:, 2:3], in_=tmps[:, 2:3], func=AF.Ln,
                    scale=-EPS_LN, bias=consts[:, 2:3]), deps=(tk,))
                tk = P.op("a", lambda e: e.activation(
                    out=tmps[:, 2:3], in_=tmps[:, 2:3], func=AF.Exp,
                    scale=-0.5), deps=(tk,))
                return P.op("a", (lambda i: (lambda e: e.mul(
                    rho(i), tmps[:, 2:3], rr(i))))(i), deps=(tk, tkr))

            def xn_fn(i):
                def f(e):
                    xr = xring[i % 3]
                    e.tensor_scalar(out=dvejunk[:], in0=xr[:], scalar1=mu(i),
                                    scalar2=None, op0=OP.subtract)
                    e.scalar_tensor_tensor(out=dvejunk[:], in0=dvejunk[:],
                                           scalar=rho(i), in1=gbc[:],
                                           op0=OP.mult, op1=OP.mult)
                    return e.tensor_add(xnbf[i % 2][:], dvejunk[:], bbc[:])
                return f

            def plan_xload(i):
                deps = ()
                if i >= 3:
                    deps = (t_stats[i - 3], t_xn[i - 3])
                t_xload[i] = P.op("s", (lambda i: (lambda e: e.dma_start(
                    out=xring[i % 3][:], in_=x_h[128 * i:128 * (i + 1), :])))(i),
                    deps=deps, inc=16, sem=f"d_x{i % 3}")

            def plan_stats_xn(i):
                t_stats[i] = plan_stats(i, (t_xload[i], t_consts))
                dep_xn = [t_stats[i], t_gbc, t_bbc]
                if i >= 2:
                    dep_xn.append(t_slab[i - 2])
                t_xn[i] = P.op("v", xn_fn(i), deps=tuple(dep_xn))

            plan_xload(0)
            plan_xload(1)
            plan_stats_xn(0)
            plan_stats_xn(1)
            plan_tabseg(0)

            def rot_fn(src_get, dst, i, ta, tb):
                def f(e):
                    src = src_get()
                    Ei = src[:, 0:ROT:2]
                    Oi = src[:, 1:ROT:2]
                    ct = costab[:, i * NPAIR:(i + 1) * NPAIR]
                    st = sintab[:, i * NPAIR:(i + 1) * NPAIR]
                    e.tensor_tensor(out=dst[:, 0:ROT:2], in0=Ei, in1=ct, op=OP.mult)
                    e.tensor_tensor(out=ta[:], in0=Oi, in1=st, op=OP.mult)
                    e.tensor_sub(dst[:, 0:ROT:2], dst[:, 0:ROT:2], ta[:])
                    e.tensor_tensor(out=dst[:, 1:ROT:2], in0=Oi, in1=ct, op=OP.mult)
                    e.tensor_tensor(out=tb[:], in0=Ei, in1=st, op=OP.mult)
                    e.tensor_add(dst[:, 1:ROT:2], dst[:, 1:ROT:2], tb[:])
                    return e.tensor_copy(out=dst[:, ROT:], in_=src[:, ROT:])
                return f

            for i in range(NT):
                if i + 2 < NT:
                    plan_xload(i + 2)

                tlast = None
                for c in range(DC):
                    depT = [t_xn[i], t_ident]
                    if c == 0 and i > 0:
                        depT.append(t_slab[i - 1])
                    tlast = P.op("t", (lambda i, c: (lambda e: e.transpose(
                        psT[:, c * 128:(c + 1) * 128],
                        xnbf[i % 2][:, c * 128:(c + 1) * 128],
                        ident[:])))(i, c), deps=tuple(depT))
                t_slab[i] = P.op("a", (lambda i: (lambda e: e.copy(
                    xnT[i % 2][:], psT[:])))(i), deps=(tlast,))

                def mm_group(i, q, ps):
                    def f(e):
                        inst = None
                        for h in range(2):
                            for c in range(DC):
                                inst = e.matmul(
                                    ps[:, h * 512:(h + 1) * 512],
                                    xnT[i % 2][:, c * 128:(c + 1) * 128],
                                    winT[q][:, c * D + h * 512:c * D + (h + 1) * 512],
                                    start=(c == 0), stop=(c == DC - 1))
                        return inst
                    return f

                dep_q = [t_slab[i], t_winT[7]]
                if i > 0:
                    dep_q.append(t_qrot[i - 1])
                t_qmm[i] = P.op("t", mm_group(i, 0, psq), deps=tuple(dep_q))

                dep_qr = [t_qmm[i], t_sinseg[i // 4], t_cosseg[i // 4]]
                if i > 0:
                    dep_qr.append(t_sqQ[i - 1])
                    dep_qr.append(t_l2nq[i - 1])
                t_qrot[i] = P.op("v", rot_fn(lambda: psq, qrot, i, tq, tq2),
                                 deps=tuple(dep_qr))

                tk = P.op("a", lambda e: e.activation(
                    out=actjunk[:], in_=qrot[:], func=AF.Square,
                    accum_out=scol[:, 2:3]), deps=(t_qrot[i],))
                tk = P.op("a", lambda e: e.activation(
                    out=tmps[:, 4:5], in_=scol[:, 2:3], func=AF.Ln,
                    bias=consts[:, 1:2]), deps=(tk,))
                t_sqQ[i] = P.op("a", (lambda i: (lambda e: e.activation(
                    out=rq_(i), in_=tmps[:, 4:5], func=AF.Exp, scale=-0.5)))(i),
                    deps=(tk,))

                t_l2nq[i] = P.op("v", (lambda i: (lambda e: e.tensor_scalar(
                    out=l2nq[:, i * D:(i + 1) * D], in0=qrot[:], scalar1=rq_(i),
                    scalar2=None, op0=OP.mult)))(i), deps=(t_sqQ[i],))

                dep_k = [t_slab[i], t_winT[15]]
                if i > 0:
                    dep_k.append(t_kraw[i - 1])
                t_kmm[i] = P.op("t", mm_group(i, 1, psk), deps=tuple(dep_k))

                dep_kw = [t_kmm[i]]
                if i > 0:
                    dep_kw.append(t_krot[i - 1])
                t_kraw[i] = P.op("a", lambda e: e.copy(kraw[:], psk[:]),
                                 deps=tuple(dep_kw))

                dep_kr = [t_kraw[i], t_sinseg[i // 4], t_cosseg[i // 4]]
                if i > 0:
                    dep_kr.append(t_kv[i - 1])
                t_krot[i] = P.op("g", rot_fn(lambda: kraw, krot, i, tk1, tk2),
                                 deps=tuple(dep_kr))

                tk = P.op("a", lambda e: e.activation(
                    out=actjunk[:], in_=krot[:], func=AF.Square,
                    accum_out=scol[:, 3:4]), deps=(t_krot[i],))
                tk = P.op("a", lambda e: e.activation(
                    out=tmps[:, 5:6], in_=scol[:, 3:4], func=AF.Ln,
                    bias=consts[:, 1:2]), deps=(tk,))
                t_sqK[i] = P.op("a", (lambda i: (lambda e: e.activation(
                    out=rk_(i), in_=tmps[:, 5:6], func=AF.Exp, scale=-0.5)))(i),
                    deps=(tk,))

                dep_v = [t_slab[i], t_winT[23]]
                if i > 0:
                    dep_v.append(t_kv[i - 1])
                t_vmm[i] = P.op("t", mm_group(i, 2, psv), deps=tuple(dep_v))

                dep_kv = [t_vmm[i], t_sqK[i]]
                if i > 0:
                    dep_kv.append(t_amm[i - 1])
                t_kv[i] = P.op("v", (lambda i: (lambda e: e.scalar_tensor_tensor(
                    out=kvbf[:], in0=krot[:], scalar=rk_(i), in1=psv[:],
                    op0=OP.mult, op1=OP.mult)))(i),
                    deps=tuple(dep_kv))

                tm0 = P.op("t", lambda e: e.matmul(
                    psA[:, :], onesbf[:], kvbf[:, 0:512], start=True, stop=True),
                    deps=(t_kv[i], t_onesbf) + ((t_amm[i - 1],) if i > 0 else (t_asb0,)))
                ta0 = P.op("v", lambda e: e.tensor_add(
                    asb[0:1, 0:512], asb[0:1, 0:512], psA[:, :]), deps=(tm0,))
                tm1 = P.op("t", lambda e: e.matmul(
                    psA[:, :], onesbf[:], kvbf[:, 512:1024], start=True, stop=True),
                    deps=(ta0,))
                t_amm[i] = P.op("v", lambda e: e.tensor_add(
                    asb[0:1, 512:1024], asb[0:1, 512:1024], psA[:, :]), deps=(tm1,))

                if i + 2 < NT:
                    plan_stats_xn(i + 2)
                if i in (0, 2, 4):
                    plan_tabseg(i // 2 + 1)

            t_ast = P.op("s", lambda e: e.dma_start(
                out=a_in.ap()[:, :], in_=asb[0:1, :]), deps=(t_amm[NT - 1],), inc=16, sem="d_ai")
            t_acc = P.op("g", lambda e: e.collective_compute(
                "AllReduce", OP.add, replica_groups=groups,
                ins=[a_in.ap().opt()], outs=[a_out.ap().opt()]),
                deps=(t_ast,), inc=1, sem="cc_a")

            P.emit_phase(nc, sems)

        with ExitStack() as sR:
            er = sR.enter_context
            w1sb = [er(nc.sbuf_tensor(f"w1sb{c}", [128, FF], BF16)) for c in range(DC)]

            t_w1 = []

            with ExitStack() as s2:
                e2 = s2.enter_context
                xring2 = [e2(nc.sbuf_tensor(f"x2r{k}", [128, D], F32)) for k in range(3)]
                x2ring = [e2(nc.sbuf_tensor(f"x2o{k}", [128, D], F32)) for k in range(2)]
                h2f = [e2(nc.sbuf_tensor(f"h2f{k}", [128, D], F32)) for k in range(2)]
                h2bf = e2(nc.sbuf_tensor("h2bf", [128, D], BF16))
                basering = [e2(nc.sbuf_tensor(f"bas{k}", [128, D], F32)) for k in range(2)]
                ybf = [e2(nc.sbuf_tensor(f"ybf{k}", [128, D], F32R)) for k in range(2)]
                yT = [e2(nc.sbuf_tensor(f"yT{k}", [128, D], F32R)) for k in range(2)]
                woutT = e2(nc.sbuf_tensor("woutT", [128, DC * D], F32R))
                Abc = e2(nc.sbuf_tensor("Abc", [128, D], F32))
                wrbc = e2(nc.sbuf_tensor("wrbc", [128, D], F32))
                psq2 = [e2(nc.psum_tensor(f"psq2{k}", [128, D], F32)) for k in range(2)]
                psT2 = [e2(nc.psum_tensor(f"psT2{k}", [128, 512], F32R)) for k in range(2)]

                t_wout = []
                for c in range(DC):
                    t_wout.append(P.op("g", (lambda c: (lambda e: e.dma_start(
                        out=woutT[:, c * D:(c + 1) * D],
                        in_=woutT_h[128 * c:128 * (c + 1), :])))(c), inc=16, sem="d_wo"))
                for c in range(DC):
                    t_w1.append(P.op("g", (lambda c: (lambda e: e.dma_start(
                        out=w1sb[c][:], in_=w1_h[128 * c:(128 * (c + 1)), :])))(c), inc=16, sem="d_w1"))
                t_abc = P.op("s", lambda e: e.dma_start(
                    out=Abc[:], in_=a_out.ap().to_broadcast([128, D])),
                    deps=(t_acc,), inc=16, sem="d_abc")
                t_wrbc = P.op("s", lambda e: e.dma_start(
                    out=wrbc[:], in_=wr_h.to_broadcast([128, D])), inc=16, sem="d_wr")

                t_x2l = [None] * NT
                t_ybf = [None] * NT
                t_yT = [None] * NT
                t_wmm = [None] * NT
                t_x2 = [None] * NT
                t_st2 = [None] * NT
                t_h2v = [None] * NT
                t_h2st = [None] * NT
                t_base = [None] * NT
                t_bst = [None] * NT

                def plan_front(i):
                    depx = ()
                    if i >= 3:
                        depx = (t_x2[i - 3],)
                    t_x2l[i] = P.op("s", (lambda i: (lambda e: e.dma_start(
                        out=xring2[i % 3][:], in_=x_h[128 * i:128 * (i + 1), :])))(i),
                        deps=depx, inc=16, sem=f"d_x{i % 3 + 3}")

                    depy = [t_abc, t_identr]
                    if i >= 2:
                        depy.append(t_yT[i - 2])
                    t_ybf[i] = P.op("v", (lambda i: (lambda e: e.tensor_tensor(
                        out=ybf[i % 2][:], in0=l2nq[:, i * D:(i + 1) * D],
                        in1=Abc[:], op=OP.mult)))(i), deps=tuple(depy))

                    evs2 = []
                    tlast2 = None
                    for c in range(DC):
                        depT = [t_ybf[i]]
                        if c % 4 == 0 and i > 0:
                            depT.append(t_yT[i - 1])
                        tlast2 = P.op("t", (lambda i, c: (lambda e: e.transpose(
                            psT2[c // 4][:, (c % 4) * 128:(c % 4 + 1) * 128],
                            ybf[i % 2][:, c * 128:(c + 1) * 128],
                            identr[:])))(i, c), deps=tuple(depT))
                        if c == 3 or c == 7:
                            evs2.append(P.op("a", (lambda i, c: (lambda e: e.copy(
                                yT[i % 2][:, (c - 3) * 128:(c + 1) * 128],
                                psT2[c // 4][:])))(i, c), deps=(tlast2,)))
                    t_yT[i] = evs2[-1]

                    def wmm_fn(i):
                        def f(e):
                            inst = None
                            for h in range(2):
                                for c in range(DC):
                                    inst = e.matmul(
                                        psq2[i % 2][:, h * 512:(h + 1) * 512],
                                        yT[i % 2][:, c * 128:(c + 1) * 128],
                                        woutT[:, c * D + h * 512:c * D + (h + 1) * 512],
                                        start=(c == 0), stop=(c == DC - 1))
                            return inst
                        return f
                    depw = [t_yT[i], t_wout[-1]]
                    if i >= 2:
                        depw.append(t_x2[i - 2])
                    t_wmm[i] = P.op("t", wmm_fn(i), deps=tuple(depw))

                def plan_back_x2(i):
                    def x2_fn(i):
                        def f(e):
                            xr = xring2[i % 3]
                            e.tensor_scalar(out=dvejunk[:], in0=xr[:], scalar1=mu(i),
                                            scalar2=None, op0=OP.subtract)
                            e.scalar_tensor_tensor(out=dvejunk[:], in0=dvejunk[:],
                                                   scalar=rr(i), in1=xr[:],
                                                   op0=OP.mult, op1=OP.add)
                            return e.tensor_add(x2ring[i % 2][:], dvejunk[:],
                                                psq2[i % 2][:])
                        return f
                    depx2 = [t_wmm[i], t_x2l[i]]
                    if i >= 2:
                        depx2.extend((t_st2[i - 2], t_h2v[i - 2], t_base[i - 2]))
                    t_x2[i] = P.op("v", x2_fn(i), deps=tuple(depx2))

                    x2t = x2ring[i % 2]
                    tk = P.op("a", (lambda x2t: (lambda e: e.activation(
                        out=actjunk[:], in_=x2t[:], func=AF.Copy,
                        accum_out=scol[:, 0:1])))(x2t), deps=(t_x2[i],))
                    tk2 = P.op("a", (lambda x2t: (lambda e: e.activation(
                        out=actjunk[:], in_=x2t[:], func=AF.Square,
                        accum_out=scol[:, 1:2])))(x2t), deps=(tk,))
                    tk = P.op("a", (lambda i: (lambda e: e.mul(
                        mu2(i), scol[:, 0:1], 1.0 / D)))(i), deps=(tk,))
                    tk = P.op("a", (lambda i: (lambda e: e.activation(
                        out=tmps[:, 0:1], in_=mu2(i), func=AF.Square)))(i),
                        deps=(tk,))
                    tk = P.op("a", lambda e: e.activation(
                        out=tmps[:, 0:1], in_=tmps[:, 0:1], func=AF.Copy,
                        scale=-1.0, bias=EPS_LN), deps=(tk,))
                    tk = P.op("a", lambda e: e.activation(
                        out=tmps[:, 1:2], in_=scol[:, 1:2], func=AF.Ln,
                        scale=1.0 / D, bias=tmps[:, 0:1]), deps=(tk, tk2))
                    t_st2[i] = P.op("a", (lambda i: (lambda e: e.activation(
                        out=rr2(i), in_=tmps[:, 1:2], func=AF.Exp, scale=-0.5)))(i),
                        deps=(tk,))

                def plan_back_h2(i):
                    def h2_fn(i):
                        def f(e):
                            e.tensor_scalar(out=h2f[i % 2][:], in0=x2ring[i % 2][:],
                                            scalar1=mu2(i), scalar2=None,
                                            op0=OP.subtract)
                            e.tensor_scalar(out=h2f[i % 2][:], in0=h2f[i % 2][:],
                                            scalar1=rr2(i), scalar2=None, op0=OP.mult)
                            return e.scalar_tensor_tensor(
                                out=dvejunk[:], in0=h2f[i % 2][:], scalar=1.0,
                                in1=wrbc[:], op0=OP.mult, op1=OP.mult,
                                accum_out=scloc[:, i:i + 1])
                        return f
                    deph = [t_st2[i], t_wrbc]
                    if i >= 2:
                        deph.extend((t_h2st[i - 2], t_base[i - 2]))
                    t_h2v[i] = P.op("v", h2_fn(i), deps=tuple(deph))

                    depc = [t_h2v[i]]
                    if i > 0:
                        depc.append(t_h2st[i - 1])
                    t_h2c = P.op("v", (lambda i: (lambda e: e.tensor_copy(
                        out=h2bf[:], in_=h2f[i % 2][:])))(i), deps=tuple(depc))
                    t_h2st[i] = P.op("s", (lambda i: (lambda e: e.dma_start(
                        out=h2tok[128 * i:128 * (i + 1), :], in_=h2bf[:])))(i),
                        deps=(t_h2c,), inc=16, sem="d_h2")

                    depb = [t_h2v[i]]
                    if i >= 2:
                        depb.append(t_bst[i - 2])
                    t_base[i] = P.op("g", (lambda i: (lambda e: e.tensor_add(
                        basering[i % 2][:], h2f[i % 2][:], x2ring[i % 2][:])))(i),
                        deps=tuple(depb))
                    t_bst[i] = P.op("g", (lambda i: (lambda e: e.dma_start(
                        out=y_h[128 * i:128 * (i + 1), :], in_=basering[i % 2][:])))(i),
                        deps=(t_base[i],), inc=16, sem=f"d_b{i % 2}")

                plan_front(0)
                plan_front(1)
                for i in range(NT):
                    plan_back_x2(i)
                    if i + 2 < NT:
                        plan_front(i + 2)
                    if i >= 1:
                        plan_back_h2(i - 1)
                plan_back_h2(NT - 1)

                t_scst = P.op("s", lambda e: e.dma_start(
                    out=sc_in.ap()[:, :], in_=scloc[:]),
                    deps=tuple(t_h2v), inc=16, sem="d_sci")
                t_cc2 = P.op("g", lambda e: e.collective_compute(
                    "AllGather", OP.bypass, replica_groups=groups,
                    ins=[sc_in.ap().opt()], outs=[sc_out.ap().opt()]),
                    deps=(t_scst,), inc=1, sem="cc_sc")
                t_scg = P.op("s", lambda e: e.dma_start(
                    out=scg[:],
                    in_=sc_out.ap().rearrange("r (p j) -> (r p) j", p=32)),
                    deps=(t_cc2,), inc=16, sem="d_scg")

                P.emit_phase(nc, sems)

            with ExitStack() as s3:
                e3 = s3.enter_context
                psC = e3(nc.psum_tensor("psC", [128, 1], F32))
                psP = e3(nc.psum_tensor("psP", [128, 1], F32))

                LO = bis[:, 0:1]
                MID = bis[:, 1:2]
                SELM = bis[:, 2:3]
                CNT = bis[:, 3:4]

                prev = P.op("v", lambda e: e.memset(LO, -16384.0), deps=(t_scg,))
                for k in range(BIS):
                    hw = float(2.0 ** (14 - k))
                    t_mid = P.op("v", (lambda hw: (lambda e: e.tensor_scalar(
                        out=MID, in0=LO, scalar1=hw, scalar2=None, op0=OP.add)))(hw),
                        deps=(prev,))
                    t_cmp = P.op("v", lambda e: e.tensor_scalar(
                        out=bmask[:], in0=scg[:], scalar1=MID, scalar2=None,
                        op0=OP.is_ge, op1=OP.add, accum_out=ccol[:]),
                        deps=(t_mid,))
                    t_cm = P.op("t", lambda e: e.matmul(
                        psC[:, :], onesf[:], ccol[:], start=True, stop=True),
                        deps=(t_cmp, t_onesf))
                    t_sel = P.op("v", lambda e: e.tensor_tensor(
                        out=SELM, in0=psC[:], in1=consts[:, 3:4], op=OP.is_ge),
                        deps=(t_cm,))
                    prev = P.op("v", (lambda hw: (lambda e: e.scalar_tensor_tensor(
                        out=LO, in0=SELM, scalar=hw, in1=LO,
                        op0=OP.mult, op1=OP.add)))(hw), deps=(t_sel,))
                t_tau = prev

                t_m = P.op("v", lambda e: e.tensor_scalar(
                    out=mloc[:], in0=scloc[:], scalar1=LO, scalar2=None,
                    op0=OP.is_ge, op1=OP.add, accum_out=rowcnt[:]), deps=(t_tau,))
                t_pb = P.op("t", lambda e: e.matmul(
                    psP[:, :], l128[:], rowcnt[:], start=True, stop=True),
                    deps=(t_m, t_l128))
                t_pbe = P.op("v", lambda e: e.tensor_copy(out=pbase[:], in_=psP[:]),
                             deps=(t_pb,))

                tk = P.op("v", lambda e: e.tensor_add(
                    pfxa[:, 1:], mloc[:, 0:NT - 1], mloc[:, 1:]), deps=(t_m,))
                tk = P.op("v", lambda e: e.tensor_copy(
                    out=pfxa[:, 0:1], in_=mloc[:, 0:1]), deps=(tk,))
                tk = P.op("v", lambda e: e.tensor_add(
                    pfxb[:, 2:], pfxa[:, 0:NT - 2], pfxa[:, 2:]), deps=(tk,))
                tk = P.op("v", lambda e: e.tensor_copy(
                    out=pfxb[:, 0:2], in_=pfxa[:, 0:2]), deps=(tk,))
                tk = P.op("v", lambda e: e.tensor_add(
                    pfxa[:, 4:], pfxb[:, 0:NT - 4], pfxb[:, 4:]), deps=(tk,))
                tk = P.op("v", lambda e: e.tensor_copy(
                    out=pfxa[:, 0:4], in_=pfxb[:, 0:4]), deps=(tk,))
                tk = P.op("v", lambda e: e.tensor_add(
                    pfxb[:, 8:], pfxa[:, 0:NT - 8], pfxa[:, 8:]), deps=(tk,))
                tk = P.op("v", lambda e: e.tensor_copy(
                    out=pfxb[:, 0:8], in_=pfxa[:, 0:8]), deps=(tk,))
                t_pfx = P.op("v", lambda e: e.tensor_sub(
                    pfxa[:], pfxb[:], mloc[:]), deps=(tk,))

                tk = P.op("v", lambda e: e.tensor_scalar(
                    out=slot[:], in0=pfxa[:], scalar1=pbase[:], scalar2=None,
                    op0=OP.add), deps=(t_pfx, t_pbe))
                tk = P.op("v", lambda e: e.tensor_scalar(
                    out=slot[:], in0=slot[:], scalar1=BIGF, scalar2=None,
                    op0=OP.add), deps=(tk,))
                tk = P.op("v", lambda e: e.scalar_tensor_tensor(
                    out=slot[:], in0=mloc[:], scalar=-BIGF, in1=slot[:],
                    op0=OP.mult, op1=OP.add), deps=(tk,))
                tk = P.op("v", lambda e: e.tensor_copy(out=dbgA[:], in_=slot[:]),
                          deps=(tk,))
                tk = P.op("v", lambda e: e.tensor_scalar(
                    out=pfxb[:], in0=slot[:], scalar1=-63.5, scalar2=1.0 / 128.0,
                    op0=OP.add, op1=OP.mult), deps=(tk,))
                tk = P.op("v", lambda e: e.tensor_copy(out=sloti[:], in_=pfxb[:]),
                          deps=(tk,))
                tk = P.op("v", lambda e: e.tensor_copy(out=pfxb[:], in_=sloti[:]),
                          deps=(tk,))
                tk = P.op("v", lambda e: e.tensor_copy(out=dbgB[:], in_=pfxb[:]),
                          deps=(tk,))
                tk = P.op("v", lambda e: e.scalar_tensor_tensor(
                    out=slot[:], in0=pfxb[:], scalar=-128.0, in1=slot[:],
                    op0=OP.mult, op1=OP.add), deps=(tk,))
                t_rank = P.op("v", lambda e: e.scalar_tensor_tensor(
                    out=slot[:], in0=slot[:], scalar=float(NG), in1=pfxb[:],
                    op0=OP.mult, op1=OP.add), deps=(tk,))
                t_spi = P.op("v", lambda e: e.tensor_copy(out=sloti[:], in_=slot[:]),
                             deps=(t_rank,))

                tkp = P.op("v", lambda e: e.tensor_copy(
                    out=pairs[:, 0:2 * NT:2], in_=scloc[:]), deps=(t_rank, t_idloc))
                t_pairs = P.op("v", lambda e: e.tensor_copy(
                    out=pairs[:, 1:2 * NT:2], in_=idloc[:]), deps=(tkp,))

                t_swid = []
                for i in range(NT):
                    t_swid.append(P.op("g", (lambda i: (lambda e: e.indirect_dma_start(
                        out=widd[:, :],
                        out_offset=IndirectOffsetOnAxis(sloti[:, i:i + 1], axis=0),
                        in_=pairs[:, 2 * i:2 * i + 2], in_offset=None,
                        bounds_check=CAP - 1, oob_is_err=False)))(i),
                        deps=(t_spi, t_pairs, t_pre_dma), inc=16, sem="d_wid"))

                t_widld = P.op("s", lambda e: e.dma_start(
                    out=widpm[:], in_=widd.rearrange("(p j) two -> p (j two)", p=128)),
                    deps=tuple(t_swid), inc=16, sem="d_wid")
                t_wslot = P.op("a", lambda e: e.activation(
                    out=wslot[:], in_=widpm[:, 0:2 * NG:2], func=AF.Sigmoid),
                    deps=(t_widld,))
                t_ids = P.op("v", lambda e: e.tensor_copy(
                    out=idsl[:], in_=widpm[:, 1:2 * NG:2]), deps=(t_widld,))

                P.emit_phase(nc, sems)

            with ExitStack() as s4:
                e4 = s4.enter_context
                w2sb = [e4(nc.sbuf_tensor(f"w2sb{c}", [128, D], BF16))
                        for c in range(FC)]
                GT = [e4(nc.sbuf_tensor(f"GT{c}", [128, CAP], BF16))
                      for c in range(DC)]
                gtok = [e4(nc.sbuf_tensor(f"gtok{k}", [128, D], BF16))
                        for k in range(2)]
                hblk = [l2nq[:, 0:FC * 256], l2nq[:, FC * 256:2 * FC * 256]]
                b2bc = e4(nc.sbuf_tensor("b2bc", [128, D], F32))
                upd = [actjunk[:], dvejunk[:]]
                psT3 = [e4(nc.psum_tensor(f"psT3{k}", [128, 512], BF16))
                        for k in range(2)]
                psH = [e4(nc.psum_tensor(f"psH{k}", [128, 256], F32))
                       for k in range(2)]
                psU = [e4(nc.psum_tensor(f"psU{k}", [128, 512], F32))
                       for k in range(4)]

                t_b2bc = P.op("s", lambda e: e.dma_start(
                    out=b2bc[:], in_=b2_h.to_broadcast([128, D])), inc=16, sem="d_b2")
                t_w2 = []
                for c in range(FC):
                    t_w2.append(P.op("g", (lambda c: (lambda e: e.dma_start(
                        out=w2sb[c][:], in_=w2_h[128 * c:128 * (c + 1), :])))(c),
                        inc=16, sem="d_w2"))

                t_gz = [P.op("v", (lambda k: (lambda e: e.memset(
                    gtok[k][:], 0.0)))(k)) for k in range(2)]
                t_gath = [None] * NG
                t_gtr = [None] * NG
                for j in range(NG):
                    depg = [t_ids, t_gz[j % 2]] + list(t_h2st)
                    if j >= 2:
                        depg.append(t_gtr[j - 2])
                    t_gath[j] = P.op("g", (lambda j: (lambda e: e.indirect_dma_start(
                        out=gtok[j % 2][:], out_offset=None,
                        in_=h2tok[:, :],
                        in_offset=IndirectOffsetOnAxis(idsl[:, j:j + 1], axis=0),
                        bounds_check=T - 1, oob_is_err=False)))(j),
                        deps=tuple(depg), inc=16, sem=f"d_ga{j % 2}")
                    evs3 = []
                    tl3 = None
                    for c in range(DC):
                        depT = [t_gath[j]]
                        if c % 4 == 0 and j > 0:
                            depT.append(t_gtr[j - 1])
                        tl3 = P.op("t", (lambda j, c: (lambda e: e.transpose(
                            psT3[c // 4][:, (c % 4) * 128:(c % 4 + 1) * 128],
                            gtok[j % 2][:, c * 128:(c + 1) * 128],
                            ident[:])))(j, c), deps=tuple(depT))
                        if c == 3 or c == 7:
                            for cc in range(c - 3, c + 1):
                                evs3.append(P.op("a", (lambda j, cc: (lambda e: e.copy(
                                    GT[cc][:, j * 128:(j + 1) * 128],
                                    psT3[cc // 4][:, (cc % 4) * 128:(cc % 4 + 1) * 128])))(j, cc),
                                    deps=(tl3,)))
                    t_gtr[j] = evs3[-1]

                gelu_hist = []
                t_ffn2_last = {}
                t_uev = {}
                t_scat = []
                for b in range(NBLK):
                    for fc in range(FC):
                        def ffn1_fn(b, fc):
                            def f(e):
                                inst = None
                                for c in range(DC):
                                    inst = e.matmul(
                                        psH[fc % 2][:, :],
                                        w1sb[c][:, fc * 128:(fc + 1) * 128],
                                        GT[c][:, b * 256:(b + 1) * 256],
                                        start=(c == 0), stop=(c == DC - 1))
                                return inst
                            return f
                        dep1 = [t_gtr[2 * b], t_gtr[2 * b + 1], t_w1[-1]]
                        if len(gelu_hist) >= 2:
                            dep1.append(gelu_hist[-2])
                        tmm = P.op("t", ffn1_fn(b, fc), deps=tuple(dep1))
                        depg2 = [tmm, t_b1]
                        if b == 0 and fc == 0:
                            depg2.append(t_ybf[NT - 1])
                        if b >= 2 and fc == 0:
                            depg2.append(t_ffn2_last[b - 2])
                        gelu_hist.append(P.op("a", (lambda b, fc: (lambda e:
                            e.activation(
                                out=hblk[b % 2][:, fc * 256:(fc + 1) * 256],
                                in_=psH[fc % 2][:], func=AF.Gelu_apprx_tanh,
                                bias=b1pm[:, fc:fc + 1])))(b, fc),
                            deps=tuple(depg2)))

                    for fc in range(FC):
                        def ffn2_fn(b, fc):
                            def f(e):
                                inst = None
                                for tc in range(2):
                                    for h in range(2):
                                        inst = e.matmul(
                                            psU[tc * 2 + h][:, :],
                                            hblk[b % 2][:, fc * 256 + tc * 128:
                                                         fc * 256 + (tc + 1) * 128],
                                            w2sb[fc][:, h * 512:(h + 1) * 512],
                                            start=(fc == 0), stop=(fc == FC - 1))
                                return inst
                            return f
                        dep2 = [gelu_hist[b * FC + fc], t_w2[-1]]
                        if b > 0 and fc == 0:
                            dep2.extend(t_uev[b - 1])
                        t_ffn2_last[b] = P.op("t", ffn2_fn(b, fc), deps=tuple(dep2))

                    uevs = []
                    for tc in range(2):
                        def uev_fn(b, tc):
                            def f(e):
                                for h in range(2):
                                    e.tensor_add(upd[tc][:, h * 512:(h + 1) * 512],
                                                 psU[tc * 2 + h][:, :],
                                                 b2bc[:, h * 512:(h + 1) * 512])
                                return e.tensor_scalar(
                                    out=upd[tc], in0=upd[tc],
                                    scalar1=wslot[:, 2 * b + tc:2 * b + tc + 1],
                                    scalar2=None, op0=OP.mult)
                            return f
                        depu = [t_ffn2_last[b], t_b2bc, t_wslot]
                        if b == 0:
                            depu.append(t_st2[NT - 1])
                        if len(t_scat) >= 2:
                            depu.append(t_scat[-2])
                        ev = P.op("v", uev_fn(b, tc), deps=tuple(depu))
                        uevs.append(ev)
                        t_scat.append(P.op("g", (lambda b, tc: (lambda e:
                            e.indirect_dma_start(
                                out=y_h[:, :],
                                out_offset=IndirectOffsetOnAxis(
                                    idsl[:, 2 * b + tc:2 * b + tc + 1], axis=0),
                                in_=upd[tc], in_offset=None,
                                bounds_check=T - 1, oob_is_err=False,
                                compute_op=OP.add)))(b, tc),
                            deps=tuple([ev] + list(t_bst)), inc=16, sem=f"d_sc{tc}"))
                    t_uev[b] = uevs

                if debug:
                    dbg_deps = tuple(t_scat) + tuple(t_bst) + (t_tau, t_wslot, t_ids, t_acc)
                    for c in range(DC):
                        P.op("g", (lambda c: (lambda e: e.dma_start(
                            out=dbg_gt[c * 128:(c + 1) * 128, :], in_=GT[c][:])))(c),
                             deps=dbg_deps + tuple(t_gtr), inc=16)
                    P.op("g", lambda e: e.dma_start(out=dbg_stats[:, :], in_=stats[:]),
                         deps=dbg_deps, inc=16)
                    P.op("g", lambda e: e.dma_start(out=dbg_sc[:, :], in_=scloc[:]),
                         deps=dbg_deps, inc=16)
                    P.op("g", lambda e: e.dma_start(out=dbg_bis[:, :], in_=bis[:]),
                         deps=dbg_deps, inc=16)
                    P.op("g", lambda e: e.dma_start(out=dbg_wid[:, :], in_=widpm[:]),
                         deps=dbg_deps, inc=16)
                    P.op("g", lambda e: e.dma_start(out=dbg_aout[:, :], in_=a_out.ap()[:, :]),
                         deps=dbg_deps, inc=16)
                    P.op("g", lambda e: e.dma_start(out=dbg_h2[:, :], in_=h2tok[:, :]),
                         deps=dbg_deps, inc=16)
                    P.op("g", lambda e: e.dma_start(out=dbg_l2nq[:, :], in_=l2nq[:]),
                         deps=dbg_deps, inc=16)
                    P.op("g", lambda e: e.dma_start(out=dbg_mloc[:, :], in_=mloc[:]),
                         deps=dbg_deps, inc=16)
                    P.op("g", lambda e: e.dma_start(out=dbg_slot[:, :], in_=slot[:]),
                         deps=dbg_deps, inc=16)
                    P.op("g", lambda e: e.dma_start(out=dbg_sloti[:, :], in_=sloti[:]),
                         deps=dbg_deps, inc=16)
                    P.op("v", lambda e: e.tensor_copy(out=bis[:, 4:5], in_=rowcnt[:]),
                         deps=dbg_deps)
                    t_dbgpb = P.op("v", lambda e: e.tensor_copy(out=bis[:, 5:6], in_=pbase[:]),
                         deps=dbg_deps)
                    P.op("g", lambda e: e.dma_start(out=dbg_rc[:, :], in_=bis[:, 4:6]),
                         deps=dbg_deps + (t_dbgpb,), inc=16)
                    P.op("g", lambda e: e.dma_start(out=dbg_widd[:, :], in_=widd[:, :]),
                         deps=dbg_deps, inc=16)
                    P.op("g", lambda e: e.dma_start(out=dbg_A2[:, :], in_=dbgA[:]),
                         deps=dbg_deps, inc=16)
                    P.op("g", lambda e: e.dma_start(out=dbg_B2[:, :], in_=dbgB[:]),
                         deps=dbg_deps, inc=16)
                    P.op("g", lambda e: e.dma_start(out=dbg_C2[:, :], in_=dbgC[:]),
                         deps=dbg_deps, inc=16)
                    P.op("g", lambda e: e.dma_start(out=dbg_pairs[:, :], in_=pairs[:]),
                         deps=dbg_deps, inc=16)
                P.op("g", lambda e: e.engine_nop(),
                     deps=tuple(t_scat) + tuple(t_bst))

                P.emit_phase(nc, sems)

    return nc


_NC_CACHE = {}


def _get_nc():
    if "nc" not in _NC_CACHE:
        _NC_CACHE["nc"] = build_nc()
    return _NC_CACHE["nc"]


def kernel(L0_attn_X, L0_attn_W_in, L0_attn_W_out, L0_attn_ln_g, L0_attn_ln_b,
           L0_ffn_w_router, L0_ffn_W1, L0_ffn_b1, L0_ffn_W2, L0_ffn_b2):
    X = np.ascontiguousarray(np.asarray(L0_attn_X, dtype=np.float32)
                             .reshape(2 * 8192, D))
    w_inT = np.ascontiguousarray(np.asarray(L0_attn_W_in, np.float32).T)
    w_outT = np.ascontiguousarray(np.asarray(L0_attn_W_out, np.float32).T)
    w1 = np.ascontiguousarray(np.asarray(L0_ffn_W1, np.float32))
    w2 = np.ascontiguousarray(np.asarray(L0_ffn_W2, np.float32))
    g_vec = np.ascontiguousarray(np.asarray(L0_attn_ln_g, np.float32)
                                 .reshape(1, D))
    b_vec = np.ascontiguousarray(np.asarray(L0_attn_ln_b, np.float32)
                                 .reshape(1, D))
    wr_vec = np.ascontiguousarray(np.asarray(L0_ffn_w_router, np.float32)
                                  .reshape(1, D))
    b2_vec = np.ascontiguousarray(np.asarray(L0_ffn_b2, np.float32).reshape(1, D))
    b1_pm = np.ascontiguousarray(np.asarray(L0_ffn_b1, np.float32)
                                 .reshape(FC, 128).T)
    identity_f32 = np.eye(128, dtype=np.float32)
    import ml_dtypes
    identity = identity_f32.astype(ml_dtypes.bfloat16)
    l128v = np.triu(np.ones((128, 128), np.float32), k=1)

    in_maps = []
    for c in range(NCORES):
        pos0 = (c % 4) * T
        post = (pos0 + 128 * np.arange(NT)[None, :] +
                np.arange(128)[:, None]).astype(np.float32)
        in_maps.append({
            "x": X[c * T:(c + 1) * T],
            "w_inT": w_inT, "w_outT": w_outT, "w1": w1, "w2": w2,
            "g_vec": g_vec, "b_vec": b_vec, "wr_vec": wr_vec, "b2_vec": b2_vec,
            "b1_pm": b1_pm, "postile": post,
            "identity": identity, "identity_f": identity_f32, "l128": l128v,
            "ones_r": np.ones((128, 1), np.float32),
        })

    nc = _get_nc()
    res = run_bass_kernel_spmd(nc, in_maps, core_ids=list(range(NCORES)))
    out = np.concatenate([res.results[c]["y"] for c in range(NCORES)], axis=0)
    return out.reshape(2, 8192, D).astype(np.float32)


# version 07
# speedup vs baseline: 1.1805x; 1.1805x over previous
import math
from contextlib import ExitStack

import numpy as np

import concourse.bass as bass
import concourse.mybir as mybir
from concourse.bass import IndirectOffsetOnAxis
from concourse.bass_utils import run_bass_kernel_spmd

F32 = mybir.dt.float32
BF16 = mybir.dt.bfloat16
I32 = mybir.dt.int32
F32R = mybir.dt.float32r
OP = mybir.AluOpType
AF = mybir.ActivationFunctionType

NCORES = 8
T = 2048
NT = 16
D = 1024
DC = 8
FF = 4096
FC = 32
ROT = 512
NPAIR = 256
CAP = 1280
NG = CAP // 128
NBLK = 5
BIS = 48
BIGF = 262144.0
PI_F = 3.1415925
INV2PI = 1.0 / (2.0 * math.pi)
TWOPI = 2.0 * math.pi
LN1E4 = math.log(10000.0)
EPS_LN = 1e-5
EPS_L2 = 1e-6

_ENG = ("s", "g", "v", "a", "t")


class Plan:

    def __init__(self):
        self.items = {e: [] for e in _ENG}
        self.done = {e: 0 for e in _ENG}
        self.count = {}

    def op(self, eng, fn, deps=(), inc=1, sem=None):
        sem = sem or ("d_" + eng if inc == 16 else "c_" + eng)
        val = self.count.get(sem, 0) + inc
        self.count[sem] = val
        self.items[eng].append((fn, tuple(deps), sem, inc))
        return (sem, val)

    def emit_phase(self, nc, sems):
        pend = {e: self.items[e][self.done[e]:] for e in _ENG}
        if not any(pend.values()):
            return
        with nc.Block() as block:
            def make_body(entries, eng_name):
                hwm = {}
                def body(e):
                    for fn, deps, sem, inc in entries:
                        for (dsem, dval) in deps:
                            if hwm.get(dsem, -1) >= dval:
                                continue
                            assert dval <= self.count.get(dsem, 0), \
                                f"wait beyond final: {dsem} {dval}"
                            e.wait_ge(sems[dsem], dval)
                            hwm[dsem] = dval
                        inst = fn(e)
                        inst.then_inc(sems[sem], inc)
                        if sem in hwm or True:
                            hwm[sem] = max(hwm.get(sem, 0), 0)
                return body

            if pend["s"]:
                block.sync(make_body(pend["s"], "s"))
            if pend["g"]:
                block.gpsimd(make_body(pend["g"], "g"))
            if pend["v"]:
                block.vector(make_body(pend["v"], "v"))
            if pend["a"]:
                block.scalar(make_body(pend["a"], "a"))
            if pend["t"]:
                block.tensor(make_body(pend["t"], "t"))
        for e in _ENG:
            self.done[e] = len(self.items[e])


def build_nc(debug=False):
    nc = bass.Bass("TRN2", target_bir_lowering=False, debug=False)

    x_h = nc.dram_tensor("x", [T, D], F32, kind="ExternalInput").ap()
    winT_h = nc.dram_tensor("w_inT", [D, 3 * D], F32, kind="ExternalInput").ap()
    woutT_h = nc.dram_tensor("w_outT", [D, D], F32, kind="ExternalInput").ap()
    w1_h = nc.dram_tensor("w1", [D, FF], F32, kind="ExternalInput").ap()
    w2_h = nc.dram_tensor("w2", [FF, D], F32, kind="ExternalInput").ap()
    g_h = nc.dram_tensor("g_vec", [1, D], F32, kind="ExternalInput").ap()
    b_h = nc.dram_tensor("b_vec", [1, D], F32, kind="ExternalInput").ap()
    wr_h = nc.dram_tensor("wr_vec", [1, D], F32, kind="ExternalInput").ap()
    b2_h = nc.dram_tensor("b2_vec", [1, D], F32, kind="ExternalInput").ap()
    b1pm_h = nc.dram_tensor("b1_pm", [128, FC], F32, kind="ExternalInput").ap()
    pos_h = nc.dram_tensor("postile", [128, NT], F32, kind="ExternalInput").ap()
    ident_h = nc.dram_tensor("identity", [128, 128], BF16, kind="ExternalInput").ap()
    identf_h = nc.dram_tensor("identity_f", [128, 128], F32R, kind="ExternalInput").ap()
    onesr_h = nc.dram_tensor("ones_r", [128, 1], F32R, kind="ExternalInput").ap()
    l128_h = nc.dram_tensor("l128", [128, 128], F32, kind="ExternalInput").ap()

    y_h = nc.dram_tensor("y", [T, D], F32, kind="ExternalOutput").ap()
    if debug:
        dbg_stats = nc.dram_tensor("dbg_stats", [128, 8 * NT], F32, kind="ExternalOutput").ap()
        dbg_sc = nc.dram_tensor("dbg_sc", [128, NT], F32, kind="ExternalOutput").ap()
        dbg_bis = nc.dram_tensor("dbg_bis", [128, 8], F32, kind="ExternalOutput").ap()
        dbg_wid = nc.dram_tensor("dbg_wid", [128, 2 * NG], F32, kind="ExternalOutput").ap()
        dbg_aout = nc.dram_tensor("dbg_aout", [1, D], F32, kind="ExternalOutput").ap()
        dbg_h2 = nc.dram_tensor("dbg_h2", [T, D], mybir.dt.bfloat16, kind="ExternalOutput").ap()
        dbg_l2nq = nc.dram_tensor("dbg_l2nq", [128, NT * D], mybir.dt.bfloat16, kind="ExternalOutput").ap()
        dbg_mloc = nc.dram_tensor("dbg_mloc", [128, NT], F32, kind="ExternalOutput").ap()
        dbg_slot = nc.dram_tensor("dbg_slot", [128, NT], F32, kind="ExternalOutput").ap()
        dbg_sloti = nc.dram_tensor("dbg_sloti", [128, NT], I32, kind="ExternalOutput").ap()
        dbg_rc = nc.dram_tensor("dbg_rc", [128, 2], F32, kind="ExternalOutput").ap()
        dbg_widd = nc.dram_tensor("dbg_widd", [CAP, 2], F32, kind="ExternalOutput").ap()
        dbg_A2 = nc.dram_tensor("dbg_A2", [128, NT], F32, kind="ExternalOutput").ap()
        dbg_B2 = nc.dram_tensor("dbg_B2", [128, NT], F32, kind="ExternalOutput").ap()
        dbg_C2 = nc.dram_tensor("dbg_C2", [128, NT], F32, kind="ExternalOutput").ap()
        dbg_pairs = nc.dram_tensor("dbg_pairs", [128, 2 * NT], F32, kind="ExternalOutput").ap()
        dbg_gt = nc.dram_tensor("dbg_gt", [DC * 128, CAP], mybir.dt.bfloat16, kind="ExternalOutput").ap()

    h2tok = nc.dram_tensor("h2tok", [T, D], BF16).ap()
    widd = nc.dram_tensor("widd", [CAP, 2], F32).ap()
    a_in = nc.dram_tensor("a_in", [1, D], F32)
    a_out = nc.dram_tensor("a_out", [1, D], F32)
    sc_in = nc.dram_tensor("sc_in", [128, NT], F32)
    sc_out = nc.dram_tensor("sc_out", [4, T], F32)
    invf_d = nc.dram_tensor("invf_d", [1, NPAIR], F32).ap()

    groups = [[0, 1, 2, 3], [4, 5, 6, 7]]
    P = Plan()

    with ExitStack() as glob:
        eg = glob.enter_context
        sems = {n: eg(nc.semaphore(n)) for n in
                ("d_s", "d_g", "c_v", "c_a", "c_t", "c_g", "cc_a", "cc_sc",
                 "d_x0", "d_x1", "d_x2", "d_x3", "d_x4", "d_x5", "d_h2",
                 "d_ga0", "d_ga1", "d_wid",
                 "d_wq", "d_wk", "d_wv", "d_wo", "d_w1", "d_w2",
                 "d_b0", "d_b1", "d_sc0", "d_sc1", "d_const", "d_abc",
                 "d_invf", "d_sci", "d_ai", "d_scg", "d_wr", "d_b2")}

        l2nq = eg(nc.sbuf_tensor("l2nq", [128, NT * D], BF16))
        stats = eg(nc.sbuf_tensor("stats", [128, 8 * NT], F32))
        mu = lambda i: stats[:, i:i + 1]
        rr = lambda i: stats[:, NT + i:NT + i + 1]
        rho = lambda i: stats[:, 2 * NT + i:2 * NT + i + 1]
        mu2 = lambda i: stats[:, 3 * NT + i:3 * NT + i + 1]
        rr2 = lambda i: stats[:, 4 * NT + i:4 * NT + i + 1]
        rq_ = lambda i: stats[:, 5 * NT + i:5 * NT + i + 1]
        rk_ = lambda i: stats[:, 6 * NT + i:6 * NT + i + 1]
        tmps = eg(nc.sbuf_tensor("tmps", [128, 8], F32))
        scol = eg(nc.sbuf_tensor("scol", [128, 4], F32))
        consts = eg(nc.sbuf_tensor("consts", [128, 4], F32))
        scloc = eg(nc.sbuf_tensor("scloc", [128, NT], F32))
        onesf = eg(nc.sbuf_tensor("onesf", [128, 128], F32))
        onesbf = eg(nc.sbuf_tensor("onesbf", [128, 1], F32R))
        ident = eg(nc.sbuf_tensor("ident", [128, 128], BF16))
        identr = eg(nc.sbuf_tensor("identr", [128, 128], F32R))
        l128 = eg(nc.sbuf_tensor("l128t", [128, 128], F32))
        b1pm = eg(nc.sbuf_tensor("b1pm", [128, FC], F32))
        postile = eg(nc.sbuf_tensor("postile_sb", [128, NT], F32))
        idloc = eg(nc.sbuf_tensor("idloc", [128, NT], F32))
        actjunk = eg(nc.sbuf_tensor("actjunk", [128, D], F32))
        dvejunk = eg(nc.sbuf_tensor("dvejunk", [128, D], F32))
        asb = eg(nc.sbuf_tensor("asb", [1, D], F32))
        scg = eg(nc.sbuf_tensor("scg", [128, 64], F32))
        bis = eg(nc.sbuf_tensor("bis", [128, 8], F32))
        bmask = eg(nc.sbuf_tensor("bmask", [128, 64], F32))
        ccol = eg(nc.sbuf_tensor("ccol", [128, 1], F32))
        mloc = eg(nc.sbuf_tensor("mloc", [128, NT], F32))
        rowcnt = eg(nc.sbuf_tensor("rowcnt", [128, 1], F32))
        pbase = eg(nc.sbuf_tensor("pbase", [128, 1], F32))
        pfxa = eg(nc.sbuf_tensor("pfxa", [128, NT], F32))
        pfxb = eg(nc.sbuf_tensor("pfxb", [128, NT], F32))
        slot = eg(nc.sbuf_tensor("slot", [128, NT], F32))
        sloti = eg(nc.sbuf_tensor("sloti", [128, NT], I32))
        pairs = eg(nc.sbuf_tensor("pairs", [128, 2 * NT], F32))
        dbgA = eg(nc.sbuf_tensor("dbgA", [128, NT], F32))
        dbgB = eg(nc.sbuf_tensor("dbgB", [128, NT], F32))
        dbgC = eg(nc.sbuf_tensor("dbgC", [128, NT], F32))
        widpre = eg(nc.sbuf_tensor("widpre", [128, 2 * NG], F32))
        widpm = eg(nc.sbuf_tensor("widpm", [128, 2 * NG], F32))
        wslot = eg(nc.sbuf_tensor("wslot", [128, NG], F32))
        idsl = eg(nc.sbuf_tensor("idsl", [128, NG], I32))

        with ExitStack() as sA:
            ea = sA.enter_context
            xring = [ea(nc.sbuf_tensor(f"xr{k}", [128, D], F32)) for k in range(3)]
            winT = [ea(nc.sbuf_tensor(f"winT{q}", [128, DC * D], BF16)) for q in range(3)]
            sintab = ea(nc.sbuf_tensor("sintab", [128, NT * NPAIR], F32))
            costab = ea(nc.sbuf_tensor("costab", [128, NT * NPAIR], F32))
            tbl = ea(nc.sbuf_tensor("tbl", [128, 4 * NPAIR], F32))
            tbli = ea(nc.sbuf_tensor("tbli", [128, 4 * NPAIR], I32))
            theta = ea(nc.sbuf_tensor("theta", [128, 4 * NPAIR], F32))
            gbc = ea(nc.sbuf_tensor("gbc", [128, D], F32))
            bbc = ea(nc.sbuf_tensor("bbc", [128, D], F32))
            invfbc = ea(nc.sbuf_tensor("invfbc", [128, NPAIR], F32))
            qrot = ea(nc.sbuf_tensor("qrot", [128, D], F32))
            krot = ea(nc.sbuf_tensor("krot", [128, D], F32))
            kraw = ea(nc.sbuf_tensor("kraw", [128, D], F32))
            kvbf = ea(nc.sbuf_tensor("kvbf", [128, D], F32R))
            tq = ea(nc.sbuf_tensor("tq", [128, NPAIR], F32))
            tq2 = ea(nc.sbuf_tensor("tq2", [128, NPAIR], F32))
            tk1 = ea(nc.sbuf_tensor("tk1", [128, NPAIR], F32))
            tk2 = ea(nc.sbuf_tensor("tk2", [128, NPAIR], F32))
            xnbf = [ea(nc.sbuf_tensor(f"xnbf{k}", [128, D], BF16)) for k in range(2)]
            xnT = [ea(nc.sbuf_tensor(f"xnT{k}", [128, D], BF16)) for k in range(2)]
            psq = ea(nc.psum_tensor("psq", [128, D], F32))
            psk = ea(nc.psum_tensor("psk", [128, D], F32))
            psv = ea(nc.psum_tensor("psv", [128, D], F32))
            psA = ea(nc.psum_tensor("psA", [1, 512], F32))
            psT = ea(nc.psum_tensor("psT", [128, D], BF16))

            t_ident = P.op("s", lambda e: e.dma_start(out=ident[:], in_=ident_h[:, :]), inc=16, sem="d_const")
            t_identr = P.op("s", lambda e: e.dma_start(out=identr[:], in_=identf_h[:, :]), inc=16, sem="d_const")
            t_l128 = P.op("s", lambda e: e.dma_start(out=l128[:], in_=l128_h[:, :]), inc=16, sem="d_const")
            t_b1 = P.op("s", lambda e: e.dma_start(out=b1pm[:], in_=b1pm_h[:, :]), inc=16, sem="d_const")
            t_pos = P.op("s", lambda e: e.dma_start(out=postile[:], in_=pos_h[:, :]), inc=16, sem="d_const")
            t_gbc = P.op("s", lambda e: e.dma_start(
                out=gbc[:], in_=g_h.to_broadcast([128, D])), inc=16, sem="d_const")
            t_bbc = P.op("s", lambda e: e.dma_start(
                out=bbc[:], in_=b_h.to_broadcast([128, D])), inc=16, sem="d_const")

            t_onesf = P.op("v", lambda e: e.memset(onesf[:], 1.0))
            P.op("v", lambda e: e.memset(consts[:, 0:1], EPS_LN))
            P.op("v", lambda e: e.memset(consts[:, 1:2], EPS_L2))
            P.op("v", lambda e: e.memset(consts[:, 2:3], 1.0 + EPS_LN))
            t_consts = P.op("v", lambda e: e.memset(consts[:, 3:4], 4096.0))
            t_onesbf = P.op("s", lambda e: e.dma_start(out=onesbf[:], in_=onesr_h[:, :]), inc=16, sem="d_const")
            t_pre1 = P.op("v", lambda e: e.memset(widpre[:], BIGF))
            t_pre2 = P.op("v", lambda e: e.memset(widpre[:, 0:2 * NG:2], 0.0),
                          deps=(t_pre1,))
            t_pre_dma = P.op("g", lambda e: e.dma_start(
                out=widd.rearrange("(p j) two -> p (j two)", p=128),
                in_=widpre[:]), deps=(t_pre2,), inc=16, sem="d_wid")
            t_asb0 = P.op("v", lambda e: e.memset(asb[:], 0.0))
            t_idloc = P.op("g", lambda e: e.iota(
                idloc[:], pattern=[[128, NT]], base=0, channel_multiplier=1,
                allow_small_or_imprecise_dtypes=True))

            t_winT = []
            for q in range(3):
                for c in range(DC):
                    t_winT.append(P.op("g", (lambda q, c: (lambda e: e.dma_start(
                        out=winT[q][:, c * D:(c + 1) * D],
                        in_=winT_h[128 * c:128 * (c + 1), q * D:(q + 1) * D])))(q, c),
                        inc=16, sem=("d_wq", "d_wk", "d_wv")[q]))

            t_jrow = P.op("g", lambda e: e.iota(
                invfbc[0:1, :], pattern=[[1, NPAIR]], base=0, channel_multiplier=0,
                allow_small_or_imprecise_dtypes=True))
            t_invf = P.op("a", lambda e: e.activation(
                out=invfbc[0:1, :], in_=invfbc[0:1, :], func=AF.Exp,
                scale=-2.0 * LN1E4 / ROT), deps=(t_jrow,))
            t_invfst = P.op("s", lambda e: e.dma_start(
                out=invf_d[:, :], in_=invfbc[0:1, :]), deps=(t_invf,), inc=16, sem="d_invf")
            t_invfbc = P.op("s", lambda e: e.dma_start(
                out=invfbc[:], in_=invf_d.to_broadcast([128, NPAIR])),
                deps=(t_invfst,), inc=16, sem="d_invf")

            SEG = 4 * NPAIR
            t_sinseg = [None] * 4
            t_cosseg = [None] * 4

            def plan_tabseg(sgi):
                pos_b = postile[:, 4 * sgi:4 * (sgi + 1)] \
                    .rearrange("p (i o) -> p i o", o=1).to_broadcast([128, 4, NPAIR])
                invf_b = invfbc[:].rearrange("p (o j) -> p o j", o=1) \
                    .to_broadcast([128, 4, NPAIR])
                th3 = theta[:].rearrange("p (i j) -> p i j", i=4)
                dep_th = [t_pos, t_invfbc]
                if sgi > 0:
                    dep_th.append(t_cosseg[sgi - 1])
                t_theta = P.op("v", (lambda pos_b, invf_b, th3: (lambda e:
                    e.tensor_tensor(out=th3, in0=pos_b, in1=invf_b, op=OP.mult)))(
                        pos_b, invf_b, th3), deps=tuple(dep_th))

                def plan_trig(dst, shift, sgi, t_theta, extra_dep=None):
                    dep1 = [t_theta] + ([extra_dep] if extra_dep else [])
                    t1 = P.op("v", (lambda shift: (lambda e: e.tensor_scalar(
                        out=tbl[:], in0=theta[:], scalar1=INV2PI,
                        scalar2=shift * INV2PI, op0=OP.mult, op1=OP.add)))(shift),
                        deps=tuple(dep1))
                    t2 = P.op("v", lambda e: e.tensor_copy(out=tbli[:], in_=tbl[:]),
                              deps=(t1,))
                    t3 = P.op("v", lambda e: e.tensor_copy(out=tbl[:], in_=tbli[:]),
                              deps=(t2,))
                    t4 = P.op("v", lambda e: e.scalar_tensor_tensor(
                        out=tbl[:], in0=tbl[:], scalar=-TWOPI, in1=theta[:],
                        op0=OP.mult, op1=OP.add), deps=(t3,))
                    if shift != 0.0:
                        t5 = P.op("v", (lambda shift: (lambda e: e.tensor_scalar(
                            out=tbl[:], in0=tbl[:], scalar1=shift, scalar2=PI_F,
                            op0=OP.add, op1=OP.min)))(shift), deps=(t4,))
                        t6 = P.op("v", lambda e: e.tensor_scalar_max(
                            tbl[:], tbl[:], -PI_F), deps=(t5,))
                    else:
                        t6 = P.op("v", lambda e: e.tensor_scalar(
                            out=tbl[:], in0=tbl[:], scalar1=PI_F, scalar2=-PI_F,
                            op0=OP.min, op1=OP.max), deps=(t4,))
                    return P.op("a", (lambda dst, sgi: (lambda e: e.activation(
                        out=dst[:, sgi * SEG:(sgi + 1) * SEG], in_=tbl[:],
                        func=AF.Sin)))(dst, sgi), deps=(t6,))

                t_sinseg[sgi] = plan_trig(sintab, 0.0, sgi, t_theta)
                t_cosseg[sgi] = plan_trig(costab, math.pi / 2.0, sgi, t_theta,
                                          extra_dep=t_sinseg[sgi])

            t_xload = [None] * (NT + 2)
            t_stats = [None] * NT
            t_xn = [None] * NT
            t_slab = [None] * NT
            t_qmm = [None] * NT
            t_kmm = [None] * NT
            t_vmm = [None] * NT
            t_qrot = [None] * NT
            t_sqQ = [None] * NT
            t_l2nq = [None] * NT
            t_kraw = [None] * NT
            t_krot = [None] * NT
            t_sqK = [None] * NT
            t_kv = [None] * NT
            t_amm = [None] * NT

            def plan_stats(i, dep0):
                xr = xring[i % 3]
                tk = P.op("a", (lambda xr: (lambda e: e.activation(
                    out=actjunk[:], in_=xr[:], func=AF.Copy,
                    accum_out=scol[:, 0:1])))(xr), deps=dep0)
                tk2 = P.op("a", (lambda xr: (lambda e: e.activation(
                    out=actjunk[:], in_=xr[:], func=AF.Square,
                    accum_out=scol[:, 1:2])))(xr), deps=(tk,))
                tk = P.op("a", (lambda i: (lambda e: e.mul(
                    mu(i), scol[:, 0:1], 1.0 / D)))(i), deps=(tk,))
                tk = P.op("a", (lambda i: (lambda e: e.activation(
                    out=tmps[:, 0:1], in_=mu(i), func=AF.Square)))(i), deps=(tk,))
                tk = P.op("a", lambda e: e.activation(
                    out=tmps[:, 0:1], in_=tmps[:, 0:1], func=AF.Copy,
                    scale=-1.0, bias=EPS_LN), deps=(tk,))
                tk = P.op("a", lambda e: e.activation(
                    out=tmps[:, 1:2], in_=scol[:, 1:2], func=AF.Ln,
                    scale=1.0 / D, bias=tmps[:, 0:1]), deps=(tk, tk2))
                tkr = P.op("a", (lambda i: (lambda e: e.activation(
                    out=rr(i), in_=tmps[:, 1:2], func=AF.Exp, scale=-0.5)))(i),
                    deps=(tk,))
                tk = P.op("a", lambda e: e.activation(
                    out=tmps[:, 2:3], in_=tmps[:, 1:2], func=AF.Exp,
                    scale=-1.0), deps=(tk,))
                tk = P.op("a", lambda e: e.activation(
                    out=tmps[:, 2:3], in_=tmps[:, 2:3], func=AF.Ln,
                    scale=-EPS_LN, bias=consts[:, 2:3]), deps=(tk,))
                tk = P.op("a", lambda e: e.activation(
                    out=tmps[:, 2:3], in_=tmps[:, 2:3], func=AF.Exp,
                    scale=-0.5), deps=(tk,))
                return P.op("a", (lambda i: (lambda e: e.mul(
                    rho(i), tmps[:, 2:3], rr(i))))(i), deps=(tk, tkr))

            def xn_fn(i):
                def f(e):
                    xr = xring[i % 3]
                    e.tensor_scalar(out=dvejunk[:], in0=xr[:], scalar1=mu(i),
                                    scalar2=None, op0=OP.subtract)
                    e.scalar_tensor_tensor(out=dvejunk[:], in0=dvejunk[:],
                                           scalar=rho(i), in1=gbc[:],
                                           op0=OP.mult, op1=OP.mult)
                    return e.tensor_add(xnbf[i % 2][:], dvejunk[:], bbc[:])
                return f

            def plan_xload(i):
                deps = ()
                if i >= 3:
                    deps = (t_stats[i - 3], t_xn[i - 3])
                t_xload[i] = P.op("s", (lambda i: (lambda e: e.dma_start(
                    out=xring[i % 3][:], in_=x_h[128 * i:128 * (i + 1), :])))(i),
                    deps=deps, inc=16, sem=f"d_x{i % 3}")

            def plan_stats_xn(i):
                t_stats[i] = plan_stats(i, (t_xload[i], t_consts))
                dep_xn = [t_stats[i], t_gbc, t_bbc]
                if i >= 2:
                    dep_xn.append(t_slab[i - 2])
                t_xn[i] = P.op("v", xn_fn(i), deps=tuple(dep_xn))

            plan_xload(0)
            plan_xload(1)
            plan_stats_xn(0)
            plan_stats_xn(1)
            plan_tabseg(0)

            def rot_fn(src_get, dst, i, ta, tb):
                def f(e):
                    src = src_get()
                    Ei = src[:, 0:ROT:2]
                    Oi = src[:, 1:ROT:2]
                    ct = costab[:, i * NPAIR:(i + 1) * NPAIR]
                    st = sintab[:, i * NPAIR:(i + 1) * NPAIR]
                    e.tensor_tensor(out=dst[:, 0:ROT:2], in0=Ei, in1=ct, op=OP.mult)
                    e.tensor_tensor(out=ta[:], in0=Oi, in1=st, op=OP.mult)
                    e.tensor_sub(dst[:, 0:ROT:2], dst[:, 0:ROT:2], ta[:])
                    e.tensor_tensor(out=dst[:, 1:ROT:2], in0=Oi, in1=ct, op=OP.mult)
                    e.tensor_tensor(out=tb[:], in0=Ei, in1=st, op=OP.mult)
                    e.tensor_add(dst[:, 1:ROT:2], dst[:, 1:ROT:2], tb[:])
                    return e.tensor_copy(out=dst[:, ROT:], in_=src[:, ROT:])
                return f

            for i in range(NT):
                if i + 2 < NT:
                    plan_xload(i + 2)

                tlast = None
                for c in range(DC):
                    depT = [t_xn[i], t_ident]
                    if c == 0 and i > 0:
                        depT.append(t_slab[i - 1])
                    tlast = P.op("t", (lambda i, c: (lambda e: e.transpose(
                        psT[:, c * 128:(c + 1) * 128],
                        xnbf[i % 2][:, c * 128:(c + 1) * 128],
                        ident[:])))(i, c), deps=tuple(depT))
                t_slab[i] = P.op("a", (lambda i: (lambda e: e.copy(
                    xnT[i % 2][:], psT[:])))(i), deps=(tlast,))

                def mm_group(i, q, ps):
                    def f(e):
                        inst = None
                        for h in range(2):
                            for c in range(DC):
                                inst = e.matmul(
                                    ps[:, h * 512:(h + 1) * 512],
                                    xnT[i % 2][:, c * 128:(c + 1) * 128],
                                    winT[q][:, c * D + h * 512:c * D + (h + 1) * 512],
                                    start=(c == 0), stop=(c == DC - 1))
                        return inst
                    return f

                dep_q = [t_slab[i], t_winT[7]]
                if i > 0:
                    dep_q.append(t_qrot[i - 1])
                t_qmm[i] = P.op("t", mm_group(i, 0, psq), deps=tuple(dep_q))

                dep_qr = [t_qmm[i], t_sinseg[i // 4], t_cosseg[i // 4]]
                if i > 0:
                    dep_qr.append(t_sqQ[i - 1])
                    dep_qr.append(t_l2nq[i - 1])
                t_qrot[i] = P.op("v", rot_fn(lambda: psq, qrot, i, tq, tq2),
                                 deps=tuple(dep_qr))

                tk = P.op("a", lambda e: e.activation(
                    out=actjunk[:], in_=qrot[:], func=AF.Square,
                    accum_out=scol[:, 2:3]), deps=(t_qrot[i],))
                tk = P.op("a", lambda e: e.activation(
                    out=tmps[:, 4:5], in_=scol[:, 2:3], func=AF.Ln,
                    bias=consts[:, 1:2]), deps=(tk,))
                t_sqQ[i] = P.op("a", (lambda i: (lambda e: e.activation(
                    out=rq_(i), in_=tmps[:, 4:5], func=AF.Exp, scale=-0.5)))(i),
                    deps=(tk,))

                t_l2nq[i] = P.op("v", (lambda i: (lambda e: e.tensor_scalar(
                    out=l2nq[:, i * D:(i + 1) * D], in0=qrot[:], scalar1=rq_(i),
                    scalar2=None, op0=OP.mult)))(i), deps=(t_sqQ[i],))

                dep_k = [t_slab[i], t_winT[15]]
                if i > 0:
                    dep_k.append(t_kraw[i - 1])
                t_kmm[i] = P.op("t", mm_group(i, 1, psk), deps=tuple(dep_k))

                dep_kw = [t_kmm[i]]
                if i > 0:
                    dep_kw.append(t_krot[i - 1])
                t_kraw[i] = P.op("a", lambda e: e.copy(kraw[:], psk[:]),
                                 deps=tuple(dep_kw))

                dep_kr = [t_kraw[i], t_sinseg[i // 4], t_cosseg[i // 4]]
                if i > 0:
                    dep_kr.append(t_kv[i - 1])
                t_krot[i] = P.op("g", rot_fn(lambda: kraw, krot, i, tk1, tk2),
                                 deps=tuple(dep_kr))

                tk = P.op("a", lambda e: e.activation(
                    out=actjunk[:], in_=krot[:], func=AF.Square,
                    accum_out=scol[:, 3:4]), deps=(t_krot[i],))
                tk = P.op("a", lambda e: e.activation(
                    out=tmps[:, 5:6], in_=scol[:, 3:4], func=AF.Ln,
                    bias=consts[:, 1:2]), deps=(tk,))
                t_sqK[i] = P.op("a", (lambda i: (lambda e: e.activation(
                    out=rk_(i), in_=tmps[:, 5:6], func=AF.Exp, scale=-0.5)))(i),
                    deps=(tk,))

                dep_v = [t_slab[i], t_winT[23]]
                if i > 0:
                    dep_v.append(t_kv[i - 1])
                t_vmm[i] = P.op("t", mm_group(i, 2, psv), deps=tuple(dep_v))

                dep_kv = [t_vmm[i], t_sqK[i]]
                if i > 0:
                    dep_kv.append(t_amm[i - 1])
                t_kv[i] = P.op("v", (lambda i: (lambda e: e.scalar_tensor_tensor(
                    out=kvbf[:], in0=krot[:], scalar=rk_(i), in1=psv[:],
                    op0=OP.mult, op1=OP.mult)))(i),
                    deps=tuple(dep_kv))

                tm0 = P.op("t", lambda e: e.matmul(
                    psA[:, :], onesbf[:], kvbf[:, 0:512], start=True, stop=True),
                    deps=(t_kv[i], t_onesbf) + ((t_amm[i - 1],) if i > 0 else (t_asb0,)))
                ta0 = P.op("v", lambda e: e.tensor_add(
                    asb[0:1, 0:512], asb[0:1, 0:512], psA[:, :]), deps=(tm0,))
                tm1 = P.op("t", lambda e: e.matmul(
                    psA[:, :], onesbf[:], kvbf[:, 512:1024], start=True, stop=True),
                    deps=(ta0,))
                t_amm[i] = P.op("v", lambda e: e.tensor_add(
                    asb[0:1, 512:1024], asb[0:1, 512:1024], psA[:, :]), deps=(tm1,))

                if i + 2 < NT:
                    plan_stats_xn(i + 2)
                if i in (0, 2, 4):
                    plan_tabseg(i // 2 + 1)

            t_ast = P.op("s", lambda e: e.dma_start(
                out=a_in.ap()[:, :], in_=asb[0:1, :]), deps=(t_amm[NT - 1],), inc=16, sem="d_ai")
            t_acc = P.op("g", lambda e: e.collective_compute(
                "AllReduce", OP.add, replica_groups=groups,
                ins=[a_in.ap().opt()], outs=[a_out.ap().opt()]),
                deps=(t_ast,), inc=1, sem="cc_a")

            P.emit_phase(nc, sems)

        with ExitStack() as sR:
            er = sR.enter_context
            w1sb = [er(nc.sbuf_tensor(f"w1sb{c}", [128, FF], BF16)) for c in range(DC)]

            t_w1 = []

            with ExitStack() as s2:
                e2 = s2.enter_context
                xring2 = [e2(nc.sbuf_tensor(f"x2r{k}", [128, D], F32)) for k in range(3)]
                x2ring = [e2(nc.sbuf_tensor(f"x2o{k}", [128, D], F32)) for k in range(2)]
                h2f = [e2(nc.sbuf_tensor(f"h2f{k}", [128, D], F32)) for k in range(2)]
                h2bf = e2(nc.sbuf_tensor("h2bf", [128, D], BF16))
                basering = [e2(nc.sbuf_tensor(f"bas{k}", [128, D], F32)) for k in range(2)]
                ybf = [e2(nc.sbuf_tensor(f"ybf{k}", [128, D], F32R)) for k in range(2)]
                yT = [e2(nc.sbuf_tensor(f"yT{k}", [128, D], F32R)) for k in range(2)]
                woutT = e2(nc.sbuf_tensor("woutT", [128, DC * D], F32R))
                Abc = e2(nc.sbuf_tensor("Abc", [128, D], F32))
                wrbc = e2(nc.sbuf_tensor("wrbc", [128, D], F32))
                psq2 = [e2(nc.psum_tensor(f"psq2{k}", [128, D], F32)) for k in range(2)]
                psT2 = [e2(nc.psum_tensor(f"psT2{k}", [128, 512], F32R)) for k in range(2)]

                t_wout = []
                for c in range(DC):
                    t_wout.append(P.op("g", (lambda c: (lambda e: e.dma_start(
                        out=woutT[:, c * D:(c + 1) * D],
                        in_=woutT_h[128 * c:128 * (c + 1), :])))(c), inc=16, sem="d_wo"))
                for c in range(DC):
                    t_w1.append(P.op("g", (lambda c: (lambda e: e.dma_start(
                        out=w1sb[c][:], in_=w1_h[128 * c:(128 * (c + 1)), :])))(c), inc=16, sem="d_w1"))
                t_abc = P.op("s", lambda e: e.dma_start(
                    out=Abc[:], in_=a_out.ap().to_broadcast([128, D])),
                    deps=(t_acc,), inc=16, sem="d_abc")
                t_wrbc = P.op("s", lambda e: e.dma_start(
                    out=wrbc[:], in_=wr_h.to_broadcast([128, D])), inc=16, sem="d_wr")

                t_x2l = [None] * NT
                t_ybf = [None] * NT
                t_yT = [None] * NT
                t_wmm = [None] * NT
                t_x2 = [None] * NT
                t_st2 = [None] * NT
                t_h2v = [None] * NT
                t_h2st = [None] * NT
                t_base = [None] * NT
                t_bst = [None] * NT

                def plan_front(i):
                    depx = ()
                    if i >= 3:
                        depx = (t_x2[i - 3],)
                    t_x2l[i] = P.op("s", (lambda i: (lambda e: e.dma_start(
                        out=xring2[i % 3][:], in_=x_h[128 * i:128 * (i + 1), :])))(i),
                        deps=depx, inc=16, sem=f"d_x{i % 3 + 3}")

                    depy = [t_abc, t_identr]
                    if i >= 2:
                        depy.append(t_yT[i - 2])
                    t_ybf[i] = P.op("v", (lambda i: (lambda e: e.tensor_tensor(
                        out=ybf[i % 2][:], in0=l2nq[:, i * D:(i + 1) * D],
                        in1=Abc[:], op=OP.mult)))(i), deps=tuple(depy))

                    evs2 = []
                    tlast2 = None
                    for c in range(DC):
                        depT = [t_ybf[i]]
                        if c % 4 == 0 and i > 0:
                            depT.append(t_yT[i - 1])
                        tlast2 = P.op("t", (lambda i, c: (lambda e: e.transpose(
                            psT2[c // 4][:, (c % 4) * 128:(c % 4 + 1) * 128],
                            ybf[i % 2][:, c * 128:(c + 1) * 128],
                            identr[:])))(i, c), deps=tuple(depT))
                        if c == 3 or c == 7:
                            evs2.append(P.op("a", (lambda i, c: (lambda e: e.copy(
                                yT[i % 2][:, (c - 3) * 128:(c + 1) * 128],
                                psT2[c // 4][:])))(i, c), deps=(tlast2,)))
                    t_yT[i] = evs2[-1]

                    def wmm_fn(i):
                        def f(e):
                            inst = None
                            for h in range(2):
                                for c in range(DC):
                                    inst = e.matmul(
                                        psq2[i % 2][:, h * 512:(h + 1) * 512],
                                        yT[i % 2][:, c * 128:(c + 1) * 128],
                                        woutT[:, c * D + h * 512:c * D + (h + 1) * 512],
                                        start=(c == 0), stop=(c == DC - 1))
                            return inst
                        return f
                    depw = [t_yT[i], t_wout[-1]]
                    if i >= 2:
                        depw.append(t_x2[i - 2])
                    t_wmm[i] = P.op("t", wmm_fn(i), deps=tuple(depw))

                def plan_back_x2(i):
                    def x2_fn(i):
                        def f(e):
                            xr = xring2[i % 3]
                            e.tensor_scalar(out=dvejunk[:], in0=xr[:], scalar1=mu(i),
                                            scalar2=None, op0=OP.subtract)
                            e.scalar_tensor_tensor(out=dvejunk[:], in0=dvejunk[:],
                                                   scalar=rr(i), in1=xr[:],
                                                   op0=OP.mult, op1=OP.add)
                            return e.tensor_add(x2ring[i % 2][:], dvejunk[:],
                                                psq2[i % 2][:])
                        return f
                    depx2 = [t_wmm[i], t_x2l[i]]
                    if i >= 2:
                        depx2.extend((t_st2[i - 2], t_h2v[i - 2], t_base[i - 2]))
                    t_x2[i] = P.op("v", x2_fn(i), deps=tuple(depx2))

                    x2t = x2ring[i % 2]
                    tk = P.op("a", (lambda x2t: (lambda e: e.activation(
                        out=actjunk[:], in_=x2t[:], func=AF.Copy,
                        accum_out=scol[:, 0:1])))(x2t), deps=(t_x2[i],))
                    tk2 = P.op("a", (lambda x2t: (lambda e: e.activation(
                        out=actjunk[:], in_=x2t[:], func=AF.Square,
                        accum_out=scol[:, 1:2])))(x2t), deps=(tk,))
                    tk = P.op("a", (lambda i: (lambda e: e.mul(
                        mu2(i), scol[:, 0:1], 1.0 / D)))(i), deps=(tk,))
                    tk = P.op("a", (lambda i: (lambda e: e.activation(
                        out=tmps[:, 0:1], in_=mu2(i), func=AF.Square)))(i),
                        deps=(tk,))
                    tk = P.op("a", lambda e: e.activation(
                        out=tmps[:, 0:1], in_=tmps[:, 0:1], func=AF.Copy,
                        scale=-1.0, bias=EPS_LN), deps=(tk,))
                    tk = P.op("a", lambda e: e.activation(
                        out=tmps[:, 1:2], in_=scol[:, 1:2], func=AF.Ln,
                        scale=1.0 / D, bias=tmps[:, 0:1]), deps=(tk, tk2))
                    t_st2[i] = P.op("a", (lambda i: (lambda e: e.activation(
                        out=rr2(i), in_=tmps[:, 1:2], func=AF.Exp, scale=-0.5)))(i),
                        deps=(tk,))

                def plan_back_h2(i):
                    def h2_fn(i):
                        def f(e):
                            e.tensor_scalar(out=h2f[i % 2][:], in0=x2ring[i % 2][:],
                                            scalar1=mu2(i), scalar2=None,
                                            op0=OP.subtract)
                            e.tensor_scalar(out=h2f[i % 2][:], in0=h2f[i % 2][:],
                                            scalar1=rr2(i), scalar2=None, op0=OP.mult)
                            return e.scalar_tensor_tensor(
                                out=dvejunk[:], in0=h2f[i % 2][:], scalar=1.0,
                                in1=wrbc[:], op0=OP.mult, op1=OP.mult,
                                accum_out=scloc[:, i:i + 1])
                        return f
                    deph = [t_st2[i], t_wrbc]
                    if i >= 2:
                        deph.extend((t_h2st[i - 2], t_base[i - 2]))
                    t_h2v[i] = P.op("v", h2_fn(i), deps=tuple(deph))

                    depc = [t_h2v[i]]
                    if i > 0:
                        depc.append(t_h2st[i - 1])
                    t_h2c = P.op("v", (lambda i: (lambda e: e.tensor_copy(
                        out=h2bf[:], in_=h2f[i % 2][:])))(i), deps=tuple(depc))
                    t_h2st[i] = P.op("s", (lambda i: (lambda e: e.dma_start(
                        out=h2tok[128 * i:128 * (i + 1), :], in_=h2bf[:])))(i),
                        deps=(t_h2c,), inc=16, sem="d_h2")

                    depb = [t_h2v[i]]
                    if i >= 2:
                        depb.append(t_bst[i - 2])
                    t_base[i] = P.op("g", (lambda i: (lambda e: e.tensor_add(
                        basering[i % 2][:], h2f[i % 2][:], x2ring[i % 2][:])))(i),
                        deps=tuple(depb))
                    t_bst[i] = P.op("g", (lambda i: (lambda e: e.dma_start(
                        out=y_h[128 * i:128 * (i + 1), :], in_=basering[i % 2][:])))(i),
                        deps=(t_base[i],), inc=16, sem=f"d_b{i % 2}")

                plan_front(0)
                plan_front(1)
                for i in range(NT):
                    plan_back_x2(i)
                    if i + 2 < NT:
                        plan_front(i + 2)
                    if i >= 1:
                        plan_back_h2(i - 1)
                plan_back_h2(NT - 1)

                t_scst = P.op("s", lambda e: e.dma_start(
                    out=sc_in.ap()[:, :], in_=scloc[:]),
                    deps=tuple(t_h2v), inc=16, sem="d_sci")
                t_cc2 = P.op("g", lambda e: e.collective_compute(
                    "AllGather", OP.bypass, replica_groups=groups,
                    ins=[sc_in.ap().opt()], outs=[sc_out.ap().opt()]),
                    deps=(t_scst,), inc=1, sem="cc_sc")
                t_scg = P.op("s", lambda e: e.dma_start(
                    out=scg[:],
                    in_=sc_out.ap().rearrange("r (p j) -> (r p) j", p=32)),
                    deps=(t_cc2,), inc=16, sem="d_scg")

                P.emit_phase(nc, sems)

            with ExitStack() as s3:
                e3 = s3.enter_context
                psC = e3(nc.psum_tensor("psC", [128, 1], F32))
                psP = e3(nc.psum_tensor("psP", [128, 1], F32))

                LO = bis[:, 0:1]
                MID = bis[:, 1:2]
                SELM = bis[:, 2:3]
                CNT = bis[:, 3:4]

                prev = P.op("v", lambda e: e.memset(LO, -16384.0), deps=(t_scg,))
                for k in range(BIS):
                    hw = float(2.0 ** (14 - k))
                    t_mid = P.op("v", (lambda hw: (lambda e: e.tensor_scalar(
                        out=MID, in0=LO, scalar1=hw, scalar2=None, op0=OP.add)))(hw),
                        deps=(prev,))
                    t_cmp = P.op("v", lambda e: e.tensor_scalar(
                        out=bmask[:], in0=scg[:], scalar1=MID, scalar2=None,
                        op0=OP.is_ge, op1=OP.add, accum_out=ccol[:]),
                        deps=(t_mid,))
                    t_cm = P.op("t", lambda e: e.matmul(
                        psC[:, :], onesf[:], ccol[:], start=True, stop=True),
                        deps=(t_cmp, t_onesf))
                    t_sel = P.op("v", lambda e: e.tensor_tensor(
                        out=SELM, in0=psC[:], in1=consts[:, 3:4], op=OP.is_ge),
                        deps=(t_cm,))
                    prev = P.op("v", (lambda hw: (lambda e: e.scalar_tensor_tensor(
                        out=LO, in0=SELM, scalar=hw, in1=LO,
                        op0=OP.mult, op1=OP.add)))(hw), deps=(t_sel,))
                t_tau = prev

                t_m = P.op("v", lambda e: e.tensor_scalar(
                    out=mloc[:], in0=scloc[:], scalar1=LO, scalar2=None,
                    op0=OP.is_ge, op1=OP.add, accum_out=rowcnt[:]), deps=(t_tau,))
                t_pb = P.op("t", lambda e: e.matmul(
                    psP[:, :], l128[:], rowcnt[:], start=True, stop=True),
                    deps=(t_m, t_l128))
                t_pbe = P.op("v", lambda e: e.tensor_copy(out=pbase[:], in_=psP[:]),
                             deps=(t_pb,))

                tk = P.op("v", lambda e: e.tensor_add(
                    pfxa[:, 1:], mloc[:, 0:NT - 1], mloc[:, 1:]), deps=(t_m,))
                tk = P.op("v", lambda e: e.tensor_copy(
                    out=pfxa[:, 0:1], in_=mloc[:, 0:1]), deps=(tk,))
                tk = P.op("v", lambda e: e.tensor_add(
                    pfxb[:, 2:], pfxa[:, 0:NT - 2], pfxa[:, 2:]), deps=(tk,))
                tk = P.op("v", lambda e: e.tensor_copy(
                    out=pfxb[:, 0:2], in_=pfxa[:, 0:2]), deps=(tk,))
                tk = P.op("v", lambda e: e.tensor_add(
                    pfxa[:, 4:], pfxb[:, 0:NT - 4], pfxb[:, 4:]), deps=(tk,))
                tk = P.op("v", lambda e: e.tensor_copy(
                    out=pfxa[:, 0:4], in_=pfxb[:, 0:4]), deps=(tk,))
                tk = P.op("v", lambda e: e.tensor_add(
                    pfxb[:, 8:], pfxa[:, 0:NT - 8], pfxa[:, 8:]), deps=(tk,))
                tk = P.op("v", lambda e: e.tensor_copy(
                    out=pfxb[:, 0:8], in_=pfxa[:, 0:8]), deps=(tk,))
                t_pfx = P.op("v", lambda e: e.tensor_sub(
                    pfxa[:], pfxb[:], mloc[:]), deps=(tk,))

                tk = P.op("v", lambda e: e.tensor_scalar(
                    out=slot[:], in0=pfxa[:], scalar1=pbase[:], scalar2=None,
                    op0=OP.add), deps=(t_pfx, t_pbe))
                tk = P.op("v", lambda e: e.tensor_scalar(
                    out=slot[:], in0=slot[:], scalar1=BIGF, scalar2=None,
                    op0=OP.add), deps=(tk,))
                tk = P.op("v", lambda e: e.scalar_tensor_tensor(
                    out=slot[:], in0=mloc[:], scalar=-BIGF, in1=slot[:],
                    op0=OP.mult, op1=OP.add), deps=(tk,))
                tk = P.op("v", lambda e: e.tensor_copy(out=dbgA[:], in_=slot[:]),
                          deps=(tk,))
                tk = P.op("v", lambda e: e.tensor_scalar(
                    out=pfxb[:], in0=slot[:], scalar1=-63.5, scalar2=1.0 / 128.0,
                    op0=OP.add, op1=OP.mult), deps=(tk,))
                tk = P.op("v", lambda e: e.tensor_copy(out=sloti[:], in_=pfxb[:]),
                          deps=(tk,))
                tk = P.op("v", lambda e: e.tensor_copy(out=pfxb[:], in_=sloti[:]),
                          deps=(tk,))
                tk = P.op("v", lambda e: e.tensor_copy(out=dbgB[:], in_=pfxb[:]),
                          deps=(tk,))
                tk = P.op("v", lambda e: e.scalar_tensor_tensor(
                    out=slot[:], in0=pfxb[:], scalar=-128.0, in1=slot[:],
                    op0=OP.mult, op1=OP.add), deps=(tk,))
                t_rank = P.op("v", lambda e: e.scalar_tensor_tensor(
                    out=slot[:], in0=slot[:], scalar=float(NG), in1=pfxb[:],
                    op0=OP.mult, op1=OP.add), deps=(tk,))
                t_spi = P.op("v", lambda e: e.tensor_copy(out=sloti[:], in_=slot[:]),
                             deps=(t_rank,))

                tkp = P.op("v", lambda e: e.tensor_copy(
                    out=pairs[:, 0:2 * NT:2], in_=scloc[:]), deps=(t_rank, t_idloc))
                t_pairs = P.op("v", lambda e: e.tensor_copy(
                    out=pairs[:, 1:2 * NT:2], in_=idloc[:]), deps=(tkp,))

                t_swid = []
                for i in range(NT):
                    t_swid.append(P.op("g", (lambda i: (lambda e: e.indirect_dma_start(
                        out=widd[:, :],
                        out_offset=IndirectOffsetOnAxis(sloti[:, i:i + 1], axis=0),
                        in_=pairs[:, 2 * i:2 * i + 2], in_offset=None,
                        bounds_check=CAP - 1, oob_is_err=False)))(i),
                        deps=(t_spi, t_pairs, t_pre_dma), inc=16, sem="d_wid"))

                t_widld = P.op("s", lambda e: e.dma_start(
                    out=widpm[:], in_=widd.rearrange("(p j) two -> p (j two)", p=128)),
                    deps=tuple(t_swid), inc=16, sem="d_wid")
                t_wslot = P.op("a", lambda e: e.activation(
                    out=wslot[:], in_=widpm[:, 0:2 * NG:2], func=AF.Sigmoid),
                    deps=(t_widld,))
                t_ids = P.op("v", lambda e: e.tensor_copy(
                    out=idsl[:], in_=widpm[:, 1:2 * NG:2]), deps=(t_widld,))

                P.emit_phase(nc, sems)

            with ExitStack() as s4:
                e4 = s4.enter_context
                w2sb = [e4(nc.sbuf_tensor(f"w2sb{c}", [128, D], BF16))
                        for c in range(FC)]
                GT = [e4(nc.sbuf_tensor(f"GT{c}", [128, CAP], BF16))
                      for c in range(DC)]
                gtok = [e4(nc.sbuf_tensor(f"gtok{k}", [128, D], BF16))
                        for k in range(2)]
                hblk = [l2nq[:, 0:FC * 256], l2nq[:, FC * 256:2 * FC * 256]]
                b2bc = e4(nc.sbuf_tensor("b2bc", [128, D], F32))
                upd = [actjunk[:], dvejunk[:]]
                psT3 = [e4(nc.psum_tensor(f"psT3{k}", [128, 512], BF16))
                        for k in range(2)]
                psH = [e4(nc.psum_tensor(f"psH{k}", [128, 256], F32))
                       for k in range(2)]
                psU = [e4(nc.psum_tensor(f"psU{k}", [128, 512], F32))
                       for k in range(4)]

                t_b2bc = P.op("s", lambda e: e.dma_start(
                    out=b2bc[:], in_=b2_h.to_broadcast([128, D])), inc=16, sem="d_b2")
                t_w2 = []
                for c in range(FC):
                    t_w2.append(P.op("g", (lambda c: (lambda e: e.dma_start(
                        out=w2sb[c][:], in_=w2_h[128 * c:128 * (c + 1), :])))(c),
                        inc=16, sem="d_w2"))

                t_gz = [P.op("v", (lambda k: (lambda e: e.memset(
                    gtok[k][:], 0.0)))(k)) for k in range(2)]
                t_gath = [None] * NG
                t_gtr = [None] * NG
                for j in range(NG):
                    depg = [t_ids, t_gz[j % 2]] + list(t_h2st)
                    if j >= 2:
                        depg.append(t_gtr[j - 2])
                    t_gath[j] = P.op("g", (lambda j: (lambda e: e.indirect_dma_start(
                        out=gtok[j % 2][:], out_offset=None,
                        in_=h2tok[:, :],
                        in_offset=IndirectOffsetOnAxis(idsl[:, j:j + 1], axis=0),
                        bounds_check=T - 1, oob_is_err=False)))(j),
                        deps=tuple(depg), inc=16, sem=f"d_ga{j % 2}")
                    evs3 = []
                    tl3 = None
                    for c in range(DC):
                        depT = [t_gath[j]]
                        if c % 4 == 0 and j > 0:
                            depT.append(t_gtr[j - 1])
                        tl3 = P.op("t", (lambda j, c: (lambda e: e.transpose(
                            psT3[c // 4][:, (c % 4) * 128:(c % 4 + 1) * 128],
                            gtok[j % 2][:, c * 128:(c + 1) * 128],
                            ident[:])))(j, c), deps=tuple(depT))
                        if c == 3 or c == 7:
                            for cc in range(c - 3, c + 1):
                                evs3.append(P.op("a", (lambda j, cc: (lambda e: e.copy(
                                    GT[cc][:, j * 128:(j + 1) * 128],
                                    psT3[cc // 4][:, (cc % 4) * 128:(cc % 4 + 1) * 128])))(j, cc),
                                    deps=(tl3,)))
                    t_gtr[j] = evs3[-1]

                gelu_hist = []
                t_ffn2_last = {}
                t_uev = {}
                t_scat = []
                for b in range(NBLK):
                    for fc in range(FC):
                        def ffn1_fn(b, fc):
                            def f(e):
                                inst = None
                                for c in range(DC):
                                    inst = e.matmul(
                                        psH[fc % 2][:, :],
                                        w1sb[c][:, fc * 128:(fc + 1) * 128],
                                        GT[c][:, b * 256:(b + 1) * 256],
                                        start=(c == 0), stop=(c == DC - 1))
                                return inst
                            return f
                        dep1 = [t_gtr[2 * b], t_gtr[2 * b + 1], t_w1[-1]]
                        if len(gelu_hist) >= 2:
                            dep1.append(gelu_hist[-2])
                        tmm = P.op("t", ffn1_fn(b, fc), deps=tuple(dep1))
                        depg2 = [tmm, t_b1]
                        if b == 0 and fc == 0:
                            depg2.append(t_ybf[NT - 1])
                        if b >= 2 and fc == 0:
                            depg2.append(t_ffn2_last[b - 2])
                        gelu_hist.append(P.op("a", (lambda b, fc: (lambda e:
                            e.activation(
                                out=hblk[b % 2][:, fc * 256:(fc + 1) * 256],
                                in_=psH[fc % 2][:], func=AF.Gelu_apprx_tanh,
                                bias=b1pm[:, fc:fc + 1])))(b, fc),
                            deps=tuple(depg2)))

                    for fc in range(FC):
                        def ffn2_fn(b, fc):
                            def f(e):
                                inst = None
                                for tc in range(2):
                                    for h in range(2):
                                        inst = e.matmul(
                                            psU[tc * 2 + h][:, :],
                                            hblk[b % 2][:, fc * 256 + tc * 128:
                                                         fc * 256 + (tc + 1) * 128],
                                            w2sb[fc][:, h * 512:(h + 1) * 512],
                                            start=(fc == 0), stop=(fc == FC - 1))
                                return inst
                            return f
                        dep2 = [gelu_hist[b * FC + fc], t_w2[-1]]
                        if b > 0 and fc == 0:
                            dep2.extend(t_uev[b - 1])
                        t_ffn2_last[b] = P.op("t", ffn2_fn(b, fc), deps=tuple(dep2))

                    uevs = []
                    for tc in range(2):
                        def uev_fn(b, tc):
                            def f(e):
                                for h in range(2):
                                    e.tensor_add(upd[tc][:, h * 512:(h + 1) * 512],
                                                 psU[tc * 2 + h][:, :],
                                                 b2bc[:, h * 512:(h + 1) * 512])
                                return e.tensor_scalar(
                                    out=upd[tc], in0=upd[tc],
                                    scalar1=wslot[:, 2 * b + tc:2 * b + tc + 1],
                                    scalar2=None, op0=OP.mult)
                            return f
                        depu = [t_ffn2_last[b], t_b2bc, t_wslot]
                        if b == 0:
                            depu.append(t_st2[NT - 1])
                        if len(t_scat) >= 2:
                            depu.append(t_scat[-2])
                        ev = P.op("v", uev_fn(b, tc), deps=tuple(depu))
                        uevs.append(ev)
                        t_scat.append(P.op("g", (lambda b, tc: (lambda e:
                            e.indirect_dma_start(
                                out=y_h[:, :],
                                out_offset=IndirectOffsetOnAxis(
                                    idsl[:, 2 * b + tc:2 * b + tc + 1], axis=0),
                                in_=upd[tc], in_offset=None,
                                bounds_check=T - 1, oob_is_err=False,
                                compute_op=OP.add)))(b, tc),
                            deps=tuple([ev] + list(t_bst)), inc=16, sem=f"d_sc{tc}"))
                    t_uev[b] = uevs

                if debug:
                    dbg_deps = tuple(t_scat) + tuple(t_bst) + (t_tau, t_wslot, t_ids, t_acc)
                    for c in range(DC):
                        P.op("g", (lambda c: (lambda e: e.dma_start(
                            out=dbg_gt[c * 128:(c + 1) * 128, :], in_=GT[c][:])))(c),
                             deps=dbg_deps + tuple(t_gtr), inc=16)
                    P.op("g", lambda e: e.dma_start(out=dbg_stats[:, :], in_=stats[:]),
                         deps=dbg_deps, inc=16)
                    P.op("g", lambda e: e.dma_start(out=dbg_sc[:, :], in_=scloc[:]),
                         deps=dbg_deps, inc=16)
                    P.op("g", lambda e: e.dma_start(out=dbg_bis[:, :], in_=bis[:]),
                         deps=dbg_deps, inc=16)
                    P.op("g", lambda e: e.dma_start(out=dbg_wid[:, :], in_=widpm[:]),
                         deps=dbg_deps, inc=16)
                    P.op("g", lambda e: e.dma_start(out=dbg_aout[:, :], in_=a_out.ap()[:, :]),
                         deps=dbg_deps, inc=16)
                    P.op("g", lambda e: e.dma_start(out=dbg_h2[:, :], in_=h2tok[:, :]),
                         deps=dbg_deps, inc=16)
                    P.op("g", lambda e: e.dma_start(out=dbg_l2nq[:, :], in_=l2nq[:]),
                         deps=dbg_deps, inc=16)
                    P.op("g", lambda e: e.dma_start(out=dbg_mloc[:, :], in_=mloc[:]),
                         deps=dbg_deps, inc=16)
                    P.op("g", lambda e: e.dma_start(out=dbg_slot[:, :], in_=slot[:]),
                         deps=dbg_deps, inc=16)
                    P.op("g", lambda e: e.dma_start(out=dbg_sloti[:, :], in_=sloti[:]),
                         deps=dbg_deps, inc=16)
                    P.op("v", lambda e: e.tensor_copy(out=bis[:, 4:5], in_=rowcnt[:]),
                         deps=dbg_deps)
                    t_dbgpb = P.op("v", lambda e: e.tensor_copy(out=bis[:, 5:6], in_=pbase[:]),
                         deps=dbg_deps)
                    P.op("g", lambda e: e.dma_start(out=dbg_rc[:, :], in_=bis[:, 4:6]),
                         deps=dbg_deps + (t_dbgpb,), inc=16)
                    P.op("g", lambda e: e.dma_start(out=dbg_widd[:, :], in_=widd[:, :]),
                         deps=dbg_deps, inc=16)
                    P.op("g", lambda e: e.dma_start(out=dbg_A2[:, :], in_=dbgA[:]),
                         deps=dbg_deps, inc=16)
                    P.op("g", lambda e: e.dma_start(out=dbg_B2[:, :], in_=dbgB[:]),
                         deps=dbg_deps, inc=16)
                    P.op("g", lambda e: e.dma_start(out=dbg_C2[:, :], in_=dbgC[:]),
                         deps=dbg_deps, inc=16)
                    P.op("g", lambda e: e.dma_start(out=dbg_pairs[:, :], in_=pairs[:]),
                         deps=dbg_deps, inc=16)
                P.op("g", lambda e: e.engine_nop(),
                     deps=tuple(t_scat) + tuple(t_bst))

                P.emit_phase(nc, sems)

    return nc


_NC_CACHE = {}


def _get_nc():
    if "nc" not in _NC_CACHE:
        _NC_CACHE["nc"] = build_nc()
    return _NC_CACHE["nc"]


def kernel(L0_attn_X, L0_attn_W_in, L0_attn_W_out, L0_attn_ln_g, L0_attn_ln_b,
           L0_ffn_w_router, L0_ffn_W1, L0_ffn_b1, L0_ffn_W2, L0_ffn_b2):
    X = np.ascontiguousarray(np.asarray(L0_attn_X, dtype=np.float32)
                             .reshape(2 * 8192, D))
    w_inT = np.ascontiguousarray(np.asarray(L0_attn_W_in, np.float32).T)
    w_outT = np.ascontiguousarray(np.asarray(L0_attn_W_out, np.float32).T)
    w1 = np.ascontiguousarray(np.asarray(L0_ffn_W1, np.float32))
    w2 = np.ascontiguousarray(np.asarray(L0_ffn_W2, np.float32))
    g_vec = np.ascontiguousarray(np.asarray(L0_attn_ln_g, np.float32)
                                 .reshape(1, D))
    b_vec = np.ascontiguousarray(np.asarray(L0_attn_ln_b, np.float32)
                                 .reshape(1, D))
    wr_vec = np.ascontiguousarray(np.asarray(L0_ffn_w_router, np.float32)
                                  .reshape(1, D))
    b2_vec = np.ascontiguousarray(np.asarray(L0_ffn_b2, np.float32).reshape(1, D))
    b1_pm = np.ascontiguousarray(np.asarray(L0_ffn_b1, np.float32)
                                 .reshape(FC, 128).T)
    identity_f32 = np.eye(128, dtype=np.float32)
    import ml_dtypes
    identity = identity_f32.astype(ml_dtypes.bfloat16)
    l128v = np.triu(np.ones((128, 128), np.float32), k=1)

    in_maps = []
    for c in range(NCORES):
        pos0 = (c % 4) * T
        post = (pos0 + 128 * np.arange(NT)[None, :] +
                np.arange(128)[:, None]).astype(np.float32)
        in_maps.append({
            "x": X[c * T:(c + 1) * T],
            "w_inT": w_inT, "w_outT": w_outT, "w1": w1, "w2": w2,
            "g_vec": g_vec, "b_vec": b_vec, "wr_vec": wr_vec, "b2_vec": b2_vec,
            "b1_pm": b1_pm, "postile": post,
            "identity": identity, "identity_f": identity_f32, "l128": l128v,
            "ones_r": np.ones((128, 1), np.float32),
        })

    nc = _get_nc()
    res = run_bass_kernel_spmd(nc, in_maps, core_ids=list(range(NCORES)))
    out = np.concatenate([res.results[c]["y"] for c in range(NCORES)], axis=0)
    return out.reshape(2, 8192, D).astype(np.float32)


# version 08 (final)
# speedup vs baseline: 1.1970x; 1.0139x over previous
import math
from contextlib import ExitStack

import numpy as np

import concourse.bass as bass
import concourse.mybir as mybir
from concourse.bass import IndirectOffsetOnAxis
from concourse.bass_utils import run_bass_kernel_spmd

F32 = mybir.dt.float32
BF16 = mybir.dt.bfloat16
I32 = mybir.dt.int32
F32R = mybir.dt.float32r
OP = mybir.AluOpType
AF = mybir.ActivationFunctionType

NCORES = 8
T = 2048
NT = 16
D = 1024
DC = 8
FF = 4096
FC = 32
ROT = 512
NPAIR = 256
CAP = 1280
NG = CAP // 128
NBLK = 5
BIS = 44
BIGF = 262144.0
PI_F = 3.1415925
INV2PI = 1.0 / (2.0 * math.pi)
TWOPI = 2.0 * math.pi
LN1E4 = math.log(10000.0)
EPS_LN = 1e-5
EPS_L2 = 1e-6

_ENG = ("s", "g", "v", "a", "t")


class Plan:

    def __init__(self):
        self.items = {e: [] for e in _ENG}
        self.done = {e: 0 for e in _ENG}
        self.count = {}

    def op(self, eng, fn, deps=(), inc=1, sem=None):
        sem = sem or ("d_" + eng if inc == 16 else "c_" + eng)
        val = self.count.get(sem, 0) + inc
        self.count[sem] = val
        self.items[eng].append((fn, tuple(deps), sem, inc))
        return (sem, val)

    def emit_phase(self, nc, sems):
        pend = {e: self.items[e][self.done[e]:] for e in _ENG}
        if not any(pend.values()):
            return
        with nc.Block() as block:
            def make_body(entries, eng_name):
                hwm = {}
                def body(e):
                    for fn, deps, sem, inc in entries:
                        for (dsem, dval) in deps:
                            if hwm.get(dsem, -1) >= dval:
                                continue
                            assert dval <= self.count.get(dsem, 0), \
                                f"wait beyond final: {dsem} {dval}"
                            e.wait_ge(sems[dsem], dval)
                            hwm[dsem] = dval
                        inst = fn(e)
                        inst.then_inc(sems[sem], inc)
                        if sem in hwm or True:
                            hwm[sem] = max(hwm.get(sem, 0), 0)
                return body

            if pend["s"]:
                block.sync(make_body(pend["s"], "s"))
            if pend["g"]:
                block.gpsimd(make_body(pend["g"], "g"))
            if pend["v"]:
                block.vector(make_body(pend["v"], "v"))
            if pend["a"]:
                block.scalar(make_body(pend["a"], "a"))
            if pend["t"]:
                block.tensor(make_body(pend["t"], "t"))
        for e in _ENG:
            self.done[e] = len(self.items[e])


def build_nc(debug=False):
    nc = bass.Bass("TRN2", target_bir_lowering=False, debug=False)

    x_h = nc.dram_tensor("x", [T, D], F32, kind="ExternalInput").ap()
    winT_h = nc.dram_tensor("w_inT", [D, 3 * D], F32, kind="ExternalInput").ap()
    woutT_h = nc.dram_tensor("w_outT", [D, D], F32, kind="ExternalInput").ap()
    w1_h = nc.dram_tensor("w1", [D, FF], F32, kind="ExternalInput").ap()
    w2_h = nc.dram_tensor("w2", [FF, D], F32, kind="ExternalInput").ap()
    g_h = nc.dram_tensor("g_vec", [1, D], F32, kind="ExternalInput").ap()
    b_h = nc.dram_tensor("b_vec", [1, D], F32, kind="ExternalInput").ap()
    wr_h = nc.dram_tensor("wr_vec", [1, D], F32, kind="ExternalInput").ap()
    b2_h = nc.dram_tensor("b2_vec", [1, D], F32, kind="ExternalInput").ap()
    b1pm_h = nc.dram_tensor("b1_pm", [128, FC], F32, kind="ExternalInput").ap()
    pos_h = nc.dram_tensor("postile", [128, NT], F32, kind="ExternalInput").ap()
    ident_h = nc.dram_tensor("identity", [128, 128], BF16, kind="ExternalInput").ap()
    identf_h = nc.dram_tensor("identity_f", [128, 128], F32R, kind="ExternalInput").ap()
    onesr_h = nc.dram_tensor("ones_r", [128, 1], F32R, kind="ExternalInput").ap()
    l128_h = nc.dram_tensor("l128", [128, 128], F32, kind="ExternalInput").ap()

    y_h = nc.dram_tensor("y", [T, D], F32, kind="ExternalOutput").ap()
    if debug:
        dbg_stats = nc.dram_tensor("dbg_stats", [128, 8 * NT], F32, kind="ExternalOutput").ap()
        dbg_sc = nc.dram_tensor("dbg_sc", [128, NT], F32, kind="ExternalOutput").ap()
        dbg_bis = nc.dram_tensor("dbg_bis", [128, 8], F32, kind="ExternalOutput").ap()
        dbg_wid = nc.dram_tensor("dbg_wid", [128, 2 * NG], F32, kind="ExternalOutput").ap()
        dbg_aout = nc.dram_tensor("dbg_aout", [1, D], F32, kind="ExternalOutput").ap()
        dbg_h2 = nc.dram_tensor("dbg_h2", [T, D], mybir.dt.bfloat16, kind="ExternalOutput").ap()
        dbg_l2nq = nc.dram_tensor("dbg_l2nq", [128, NT * D], mybir.dt.bfloat16, kind="ExternalOutput").ap()
        dbg_mloc = nc.dram_tensor("dbg_mloc", [128, NT], F32, kind="ExternalOutput").ap()
        dbg_slot = nc.dram_tensor("dbg_slot", [128, NT], F32, kind="ExternalOutput").ap()
        dbg_sloti = nc.dram_tensor("dbg_sloti", [128, NT], I32, kind="ExternalOutput").ap()
        dbg_rc = nc.dram_tensor("dbg_rc", [128, 2], F32, kind="ExternalOutput").ap()
        dbg_widd = nc.dram_tensor("dbg_widd", [CAP, 2], F32, kind="ExternalOutput").ap()
        dbg_A2 = nc.dram_tensor("dbg_A2", [128, NT], F32, kind="ExternalOutput").ap()
        dbg_B2 = nc.dram_tensor("dbg_B2", [128, NT], F32, kind="ExternalOutput").ap()
        dbg_C2 = nc.dram_tensor("dbg_C2", [128, NT], F32, kind="ExternalOutput").ap()
        dbg_pairs = nc.dram_tensor("dbg_pairs", [128, 2 * NT], F32, kind="ExternalOutput").ap()
        dbg_gt = nc.dram_tensor("dbg_gt", [DC * 128, CAP], mybir.dt.bfloat16, kind="ExternalOutput").ap()

    h2tok = nc.dram_tensor("h2tok", [T, D], BF16).ap()
    widd = nc.dram_tensor("widd", [CAP, 2], F32).ap()
    a_in = nc.dram_tensor("a_in", [1, D], F32)
    a_out = nc.dram_tensor("a_out", [1, D], F32)
    sc_in = nc.dram_tensor("sc_in", [128, NT], F32)
    sc_out = nc.dram_tensor("sc_out", [4, T], F32)
    invf_d = nc.dram_tensor("invf_d", [1, NPAIR], F32).ap()

    groups = [[0, 1, 2, 3], [4, 5, 6, 7]]
    P = Plan()

    with ExitStack() as glob:
        eg = glob.enter_context
        sems = {n: eg(nc.semaphore(n)) for n in
                ("d_s", "d_g", "c_v", "c_a", "c_t", "c_g", "cc_a", "cc_sc",
                 "d_x0", "d_x1", "d_x2", "d_x3", "d_x4", "d_x5", "d_h2",
                 "d_ga0", "d_ga1", "d_wid",
                 "d_wq", "d_wk", "d_wv", "d_wo", "d_w1", "d_w2",
                 "d_b0", "d_b1", "d_sc0", "d_sc1", "d_const", "d_abc",
                 "d_invf", "d_sci", "d_ai", "d_scg", "d_wr", "d_b2")}

        l2nq = eg(nc.sbuf_tensor("l2nq", [128, NT * D], BF16))
        stats = eg(nc.sbuf_tensor("stats", [128, 8 * NT], F32))
        mu = lambda i: stats[:, i:i + 1]
        rr = lambda i: stats[:, NT + i:NT + i + 1]
        rho = lambda i: stats[:, 2 * NT + i:2 * NT + i + 1]
        mu2 = lambda i: stats[:, 3 * NT + i:3 * NT + i + 1]
        rr2 = lambda i: stats[:, 4 * NT + i:4 * NT + i + 1]
        rq_ = lambda i: stats[:, 5 * NT + i:5 * NT + i + 1]
        rk_ = lambda i: stats[:, 6 * NT + i:6 * NT + i + 1]
        tmps = eg(nc.sbuf_tensor("tmps", [128, 8], F32))
        scol = eg(nc.sbuf_tensor("scol", [128, 4], F32))
        consts = eg(nc.sbuf_tensor("consts", [128, 4], F32))
        scloc = eg(nc.sbuf_tensor("scloc", [128, NT], F32))
        onesf = eg(nc.sbuf_tensor("onesf", [128, 128], F32))
        onesbf = eg(nc.sbuf_tensor("onesbf", [128, 1], F32R))
        ident = eg(nc.sbuf_tensor("ident", [128, 128], BF16))
        identr = eg(nc.sbuf_tensor("identr", [128, 128], F32R))
        l128 = eg(nc.sbuf_tensor("l128t", [128, 128], F32))
        b1pm = eg(nc.sbuf_tensor("b1pm", [128, FC], F32))
        postile = eg(nc.sbuf_tensor("postile_sb", [128, NT], F32))
        idloc = eg(nc.sbuf_tensor("idloc", [128, NT], F32))
        actjunk = eg(nc.sbuf_tensor("actjunk", [128, D], F32))
        dvejunk = eg(nc.sbuf_tensor("dvejunk", [128, D], F32))
        asb = eg(nc.sbuf_tensor("asb", [1, D], F32))
        scg = eg(nc.sbuf_tensor("scg", [128, 64], F32))
        bis = eg(nc.sbuf_tensor("bis", [128, 8], F32))
        bmask = eg(nc.sbuf_tensor("bmask", [128, 64], F32))
        ccol = eg(nc.sbuf_tensor("ccol", [128, 1], F32))
        mloc = eg(nc.sbuf_tensor("mloc", [128, NT], F32))
        rowcnt = eg(nc.sbuf_tensor("rowcnt", [128, 1], F32))
        pbase = eg(nc.sbuf_tensor("pbase", [128, 1], F32))
        pfxa = eg(nc.sbuf_tensor("pfxa", [128, NT], F32))
        pfxb = eg(nc.sbuf_tensor("pfxb", [128, NT], F32))
        slot = eg(nc.sbuf_tensor("slot", [128, NT], F32))
        sloti = eg(nc.sbuf_tensor("sloti", [128, NT], I32))
        pairs = eg(nc.sbuf_tensor("pairs", [128, 2 * NT], F32))
        dbgA = eg(nc.sbuf_tensor("dbgA", [128, NT], F32))
        dbgB = eg(nc.sbuf_tensor("dbgB", [128, NT], F32))
        dbgC = eg(nc.sbuf_tensor("dbgC", [128, NT], F32))
        widpre = eg(nc.sbuf_tensor("widpre", [128, 2 * NG], F32))
        widpm = eg(nc.sbuf_tensor("widpm", [128, 2 * NG], F32))
        wslot = eg(nc.sbuf_tensor("wslot", [128, NG], F32))
        idsl = eg(nc.sbuf_tensor("idsl", [128, NG], I32))

        with ExitStack() as sA:
            ea = sA.enter_context
            xring = [ea(nc.sbuf_tensor(f"xr{k}", [128, D], F32)) for k in range(3)]
            winT = [ea(nc.sbuf_tensor(f"winT{q}", [128, DC * D], BF16)) for q in range(3)]
            sintab = ea(nc.sbuf_tensor("sintab", [128, NT * NPAIR], F32))
            costab = ea(nc.sbuf_tensor("costab", [128, NT * NPAIR], F32))
            tbl = ea(nc.sbuf_tensor("tbl", [128, 4 * NPAIR], F32))
            tbli = ea(nc.sbuf_tensor("tbli", [128, 4 * NPAIR], I32))
            theta = ea(nc.sbuf_tensor("theta", [128, 4 * NPAIR], F32))
            gbc = ea(nc.sbuf_tensor("gbc", [128, D], F32))
            bbc = ea(nc.sbuf_tensor("bbc", [128, D], F32))
            invfbc = ea(nc.sbuf_tensor("invfbc", [128, NPAIR], F32))
            qrot = ea(nc.sbuf_tensor("qrot", [128, D], F32))
            krot = ea(nc.sbuf_tensor("krot", [128, D], F32))
            kraw = ea(nc.sbuf_tensor("kraw", [128, D], F32))
            kvbf = ea(nc.sbuf_tensor("kvbf", [128, D], F32R))
            tq = ea(nc.sbuf_tensor("tq", [128, NPAIR], F32))
            tq2 = ea(nc.sbuf_tensor("tq2", [128, NPAIR], F32))
            tk1 = ea(nc.sbuf_tensor("tk1", [128, NPAIR], F32))
            tk2 = ea(nc.sbuf_tensor("tk2", [128, NPAIR], F32))
            xnbf = [ea(nc.sbuf_tensor(f"xnbf{k}", [128, D], BF16)) for k in range(2)]
            xnT = [ea(nc.sbuf_tensor(f"xnT{k}", [128, D], BF16)) for k in range(2)]
            psq = ea(nc.psum_tensor("psq", [128, D], F32))
            psk = ea(nc.psum_tensor("psk", [128, D], F32))
            psv = ea(nc.psum_tensor("psv", [128, D], F32))
            psA = ea(nc.psum_tensor("psA", [1, 512], F32))
            psT = ea(nc.psum_tensor("psT", [128, D], BF16))

            t_ident = P.op("s", lambda e: e.dma_start(out=ident[:], in_=ident_h[:, :]), inc=16, sem="d_const")
            t_identr = P.op("s", lambda e: e.dma_start(out=identr[:], in_=identf_h[:, :]), inc=16, sem="d_const")
            t_l128 = P.op("s", lambda e: e.dma_start(out=l128[:], in_=l128_h[:, :]), inc=16, sem="d_const")
            t_b1 = P.op("s", lambda e: e.dma_start(out=b1pm[:], in_=b1pm_h[:, :]), inc=16, sem="d_const")
            t_pos = P.op("s", lambda e: e.dma_start(out=postile[:], in_=pos_h[:, :]), inc=16, sem="d_const")
            t_gbc = P.op("s", lambda e: e.dma_start(
                out=gbc[:], in_=g_h.to_broadcast([128, D])), inc=16, sem="d_const")
            t_bbc = P.op("s", lambda e: e.dma_start(
                out=bbc[:], in_=b_h.to_broadcast([128, D])), inc=16, sem="d_const")

            t_onesf = P.op("v", lambda e: e.memset(onesf[:], 1.0))
            P.op("v", lambda e: e.memset(consts[:, 0:1], EPS_LN))
            P.op("v", lambda e: e.memset(consts[:, 1:2], EPS_L2))
            P.op("v", lambda e: e.memset(consts[:, 2:3], 1.0 + EPS_LN))
            t_consts = P.op("v", lambda e: e.memset(consts[:, 3:4], 4096.0))
            t_onesbf = P.op("s", lambda e: e.dma_start(out=onesbf[:], in_=onesr_h[:, :]), inc=16, sem="d_const")
            t_pre1 = P.op("v", lambda e: e.memset(widpre[:], BIGF))
            t_pre2 = P.op("v", lambda e: e.memset(widpre[:, 0:2 * NG:2], 0.0),
                          deps=(t_pre1,))
            t_pre_dma = P.op("g", lambda e: e.dma_start(
                out=widd.rearrange("(p j) two -> p (j two)", p=128),
                in_=widpre[:]), deps=(t_pre2,), inc=16, sem="d_wid")
            t_asb0 = P.op("v", lambda e: e.memset(asb[:], 0.0))
            t_idloc = P.op("g", lambda e: e.iota(
                idloc[:], pattern=[[128, NT]], base=0, channel_multiplier=1,
                allow_small_or_imprecise_dtypes=True))

            t_winT = []
            for q in range(3):
                for c in range(DC):
                    t_winT.append(P.op("g", (lambda q, c: (lambda e: e.dma_start(
                        out=winT[q][:, c * D:(c + 1) * D],
                        in_=winT_h[128 * c:128 * (c + 1), q * D:(q + 1) * D])))(q, c),
                        inc=16, sem=("d_wq", "d_wk", "d_wv")[q]))

            t_jrow = P.op("g", lambda e: e.iota(
                invfbc[0:1, :], pattern=[[1, NPAIR]], base=0, channel_multiplier=0,
                allow_small_or_imprecise_dtypes=True))
            t_invf = P.op("a", lambda e: e.activation(
                out=invfbc[0:1, :], in_=invfbc[0:1, :], func=AF.Exp,
                scale=-2.0 * LN1E4 / ROT), deps=(t_jrow,))
            t_invfst = P.op("s", lambda e: e.dma_start(
                out=invf_d[:, :], in_=invfbc[0:1, :]), deps=(t_invf,), inc=16, sem="d_invf")
            t_invfbc = P.op("s", lambda e: e.dma_start(
                out=invfbc[:], in_=invf_d.to_broadcast([128, NPAIR])),
                deps=(t_invfst,), inc=16, sem="d_invf")

            SEG = 4 * NPAIR
            t_sinseg = [None] * 4
            t_cosseg = [None] * 4

            def plan_tabseg(sgi):
                pos_b = postile[:, 4 * sgi:4 * (sgi + 1)] \
                    .rearrange("p (i o) -> p i o", o=1).to_broadcast([128, 4, NPAIR])
                invf_b = invfbc[:].rearrange("p (o j) -> p o j", o=1) \
                    .to_broadcast([128, 4, NPAIR])
                th3 = theta[:].rearrange("p (i j) -> p i j", i=4)
                dep_th = [t_pos, t_invfbc]
                if sgi > 0:
                    dep_th.append(t_cosseg[sgi - 1])
                t_theta = P.op("v", (lambda pos_b, invf_b, th3: (lambda e:
                    e.tensor_tensor(out=th3, in0=pos_b, in1=invf_b, op=OP.mult)))(
                        pos_b, invf_b, th3), deps=tuple(dep_th))

                def plan_trig(dst, shift, sgi, t_theta, extra_dep=None):
                    dep1 = [t_theta] + ([extra_dep] if extra_dep else [])
                    t1 = P.op("v", (lambda shift: (lambda e: e.tensor_scalar(
                        out=tbl[:], in0=theta[:], scalar1=INV2PI,
                        scalar2=shift * INV2PI, op0=OP.mult, op1=OP.add)))(shift),
                        deps=tuple(dep1))
                    t2 = P.op("v", lambda e: e.tensor_copy(out=tbli[:], in_=tbl[:]),
                              deps=(t1,))
                    t3 = P.op("v", lambda e: e.tensor_copy(out=tbl[:], in_=tbli[:]),
                              deps=(t2,))
                    t4 = P.op("v", lambda e: e.scalar_tensor_tensor(
                        out=tbl[:], in0=tbl[:], scalar=-TWOPI, in1=theta[:],
                        op0=OP.mult, op1=OP.add), deps=(t3,))
                    if shift != 0.0:
                        t5 = P.op("v", (lambda shift: (lambda e: e.tensor_scalar(
                            out=tbl[:], in0=tbl[:], scalar1=shift, scalar2=PI_F,
                            op0=OP.add, op1=OP.min)))(shift), deps=(t4,))
                        t6 = P.op("v", lambda e: e.tensor_scalar_max(
                            tbl[:], tbl[:], -PI_F), deps=(t5,))
                    else:
                        t6 = P.op("v", lambda e: e.tensor_scalar(
                            out=tbl[:], in0=tbl[:], scalar1=PI_F, scalar2=-PI_F,
                            op0=OP.min, op1=OP.max), deps=(t4,))
                    return P.op("a", (lambda dst, sgi: (lambda e: e.activation(
                        out=dst[:, sgi * SEG:(sgi + 1) * SEG], in_=tbl[:],
                        func=AF.Sin)))(dst, sgi), deps=(t6,))

                t_sinseg[sgi] = plan_trig(sintab, 0.0, sgi, t_theta)
                t_cosseg[sgi] = plan_trig(costab, math.pi / 2.0, sgi, t_theta,
                                          extra_dep=t_sinseg[sgi])

            t_xload = [None] * (NT + 2)
            t_stats = [None] * NT
            t_xn = [None] * NT
            t_slab = [None] * NT
            t_qmm = [None] * NT
            t_kmm = [None] * NT
            t_vmm = [None] * NT
            t_qrot = [None] * NT
            t_sqQ = [None] * NT
            t_l2nq = [None] * NT
            t_kraw = [None] * NT
            t_krot = [None] * NT
            t_sqK = [None] * NT
            t_kv = [None] * NT
            t_amm = [None] * NT

            def plan_stats(i, dep0):
                xr = xring[i % 3]
                tk = P.op("a", (lambda xr: (lambda e: e.activation(
                    out=actjunk[:], in_=xr[:], func=AF.Copy,
                    accum_out=scol[:, 0:1])))(xr), deps=dep0)
                tk2 = P.op("a", (lambda xr: (lambda e: e.activation(
                    out=actjunk[:], in_=xr[:], func=AF.Square,
                    accum_out=scol[:, 1:2])))(xr), deps=(tk,))
                tk = P.op("a", (lambda i: (lambda e: e.mul(
                    mu(i), scol[:, 0:1], 1.0 / D)))(i), deps=(tk,))
                tk = P.op("a", (lambda i: (lambda e: e.activation(
                    out=tmps[:, 0:1], in_=mu(i), func=AF.Square)))(i), deps=(tk,))
                tk = P.op("a", lambda e: e.activation(
                    out=tmps[:, 0:1], in_=tmps[:, 0:1], func=AF.Copy,
                    scale=-1.0, bias=EPS_LN), deps=(tk,))
                tk = P.op("a", lambda e: e.activation(
                    out=tmps[:, 1:2], in_=scol[:, 1:2], func=AF.Ln,
                    scale=1.0 / D, bias=tmps[:, 0:1]), deps=(tk, tk2))
                tkr = P.op("a", (lambda i: (lambda e: e.activation(
                    out=rr(i), in_=tmps[:, 1:2], func=AF.Exp, scale=-0.5)))(i),
                    deps=(tk,))
                tk = P.op("a", lambda e: e.activation(
                    out=tmps[:, 2:3], in_=tmps[:, 1:2], func=AF.Exp,
                    scale=-1.0), deps=(tk,))
                tk = P.op("a", lambda e: e.activation(
                    out=tmps[:, 2:3], in_=tmps[:, 2:3], func=AF.Ln,
                    scale=-EPS_LN, bias=consts[:, 2:3]), deps=(tk,))
                tk = P.op("a", lambda e: e.activation(
                    out=tmps[:, 2:3], in_=tmps[:, 2:3], func=AF.Exp,
                    scale=-0.5), deps=(tk,))
                return P.op("a", (lambda i: (lambda e: e.mul(
                    rho(i), tmps[:, 2:3], rr(i))))(i), deps=(tk, tkr))

            def xn_fn(i):
                def f(e):
                    xr = xring[i % 3]
                    e.tensor_scalar(out=dvejunk[:], in0=xr[:], scalar1=mu(i),
                                    scalar2=None, op0=OP.subtract)
                    e.scalar_tensor_tensor(out=dvejunk[:], in0=dvejunk[:],
                                           scalar=rho(i), in1=gbc[:],
                                           op0=OP.mult, op1=OP.mult)
                    return e.tensor_add(xnbf[i % 2][:], dvejunk[:], bbc[:])
                return f

            def plan_xload(i):
                deps = ()
                if i >= 3:
                    deps = (t_stats[i - 3], t_xn[i - 3])
                t_xload[i] = P.op("s", (lambda i: (lambda e: e.dma_start(
                    out=xring[i % 3][:], in_=x_h[128 * i:128 * (i + 1), :])))(i),
                    deps=deps, inc=16, sem=f"d_x{i % 3}")

            def plan_stats_xn(i):
                t_stats[i] = plan_stats(i, (t_xload[i], t_consts))
                dep_xn = [t_stats[i], t_gbc, t_bbc]
                if i >= 2:
                    dep_xn.append(t_slab[i - 2])
                t_xn[i] = P.op("v", xn_fn(i), deps=tuple(dep_xn))

            plan_xload(0)
            plan_xload(1)
            plan_stats_xn(0)
            plan_stats_xn(1)
            plan_tabseg(0)

            def rot_fn(src_get, dst, i, ta, tb):
                def f(e):
                    src = src_get()
                    Ei = src[:, 0:ROT:2]
                    Oi = src[:, 1:ROT:2]
                    ct = costab[:, i * NPAIR:(i + 1) * NPAIR]
                    st = sintab[:, i * NPAIR:(i + 1) * NPAIR]
                    e.tensor_tensor(out=dst[:, 0:ROT:2], in0=Ei, in1=ct, op=OP.mult)
                    e.tensor_tensor(out=ta[:], in0=Oi, in1=st, op=OP.mult)
                    e.tensor_sub(dst[:, 0:ROT:2], dst[:, 0:ROT:2], ta[:])
                    e.tensor_tensor(out=dst[:, 1:ROT:2], in0=Oi, in1=ct, op=OP.mult)
                    e.tensor_tensor(out=tb[:], in0=Ei, in1=st, op=OP.mult)
                    e.tensor_add(dst[:, 1:ROT:2], dst[:, 1:ROT:2], tb[:])
                    return e.tensor_copy(out=dst[:, ROT:], in_=src[:, ROT:])
                return f

            for i in range(NT):
                if i + 2 < NT:
                    plan_xload(i + 2)

                tlast = None
                for c in range(DC):
                    depT = [t_xn[i], t_ident]
                    if c == 0 and i > 0:
                        depT.append(t_slab[i - 1])
                    tlast = P.op("t", (lambda i, c: (lambda e: e.transpose(
                        psT[:, c * 128:(c + 1) * 128],
                        xnbf[i % 2][:, c * 128:(c + 1) * 128],
                        ident[:])))(i, c), deps=tuple(depT))
                t_slab[i] = P.op("a", (lambda i: (lambda e: e.copy(
                    xnT[i % 2][:], psT[:])))(i), deps=(tlast,))

                def mm_group(i, q, ps):
                    def f(e):
                        inst = None
                        for c in range(DC):
                            for h in range(2):
                                inst = e.matmul(
                                    ps[:, h * 512:(h + 1) * 512],
                                    xnT[i % 2][:, c * 128:(c + 1) * 128],
                                    winT[q][:, c * D + h * 512:c * D + (h + 1) * 512],
                                    start=(c == 0), stop=(c == DC - 1))
                        return inst
                    return f

                dep_q = [t_slab[i], t_winT[7]]
                if i > 0:
                    dep_q.append(t_qrot[i - 1])
                t_qmm[i] = P.op("t", mm_group(i, 0, psq), deps=tuple(dep_q))

                dep_qr = [t_qmm[i], t_sinseg[i // 4], t_cosseg[i // 4]]
                if i > 0:
                    dep_qr.append(t_sqQ[i - 1])
                    dep_qr.append(t_l2nq[i - 1])
                t_qrot[i] = P.op("v", rot_fn(lambda: psq, qrot, i, tq, tq2),
                                 deps=tuple(dep_qr))

                tk = P.op("a", lambda e: e.activation(
                    out=actjunk[:], in_=qrot[:], func=AF.Square,
                    accum_out=scol[:, 2:3]), deps=(t_qrot[i],))
                tk = P.op("a", lambda e: e.activation(
                    out=tmps[:, 4:5], in_=scol[:, 2:3], func=AF.Ln,
                    bias=consts[:, 1:2]), deps=(tk,))
                t_sqQ[i] = P.op("a", (lambda i: (lambda e: e.activation(
                    out=rq_(i), in_=tmps[:, 4:5], func=AF.Exp, scale=-0.5)))(i),
                    deps=(tk,))

                t_l2nq[i] = P.op("v", (lambda i: (lambda e: e.tensor_scalar(
                    out=l2nq[:, i * D:(i + 1) * D], in0=qrot[:], scalar1=rq_(i),
                    scalar2=None, op0=OP.mult)))(i), deps=(t_sqQ[i],))

                dep_k = [t_slab[i], t_winT[15]]
                if i > 0:
                    dep_k.append(t_kraw[i - 1])
                t_kmm[i] = P.op("t", mm_group(i, 1, psk), deps=tuple(dep_k))

                dep_kw = [t_kmm[i]]
                if i > 0:
                    dep_kw.append(t_krot[i - 1])
                t_kraw[i] = P.op("a", lambda e: e.copy(kraw[:], psk[:]),
                                 deps=tuple(dep_kw))

                dep_kr = [t_kraw[i], t_sinseg[i // 4], t_cosseg[i // 4]]
                if i > 0:
                    dep_kr.append(t_kv[i - 1])
                t_krot[i] = P.op("g", rot_fn(lambda: kraw, krot, i, tk1, tk2),
                                 deps=tuple(dep_kr))

                tk = P.op("a", lambda e: e.activation(
                    out=actjunk[:], in_=krot[:], func=AF.Square,
                    accum_out=scol[:, 3:4]), deps=(t_krot[i],))
                tk = P.op("a", lambda e: e.activation(
                    out=tmps[:, 5:6], in_=scol[:, 3:4], func=AF.Ln,
                    bias=consts[:, 1:2]), deps=(tk,))
                t_sqK[i] = P.op("a", (lambda i: (lambda e: e.activation(
                    out=rk_(i), in_=tmps[:, 5:6], func=AF.Exp, scale=-0.5)))(i),
                    deps=(tk,))

                dep_v = [t_slab[i], t_winT[23]]
                if i > 0:
                    dep_v.append(t_kv[i - 1])
                t_vmm[i] = P.op("t", mm_group(i, 2, psv), deps=tuple(dep_v))

                dep_kv = [t_vmm[i], t_sqK[i]]
                if i > 0:
                    dep_kv.append(t_amm[i - 1])
                t_kv[i] = P.op("v", (lambda i: (lambda e: e.scalar_tensor_tensor(
                    out=kvbf[:], in0=krot[:], scalar=rk_(i), in1=psv[:],
                    op0=OP.mult, op1=OP.mult)))(i),
                    deps=tuple(dep_kv))

                tm0 = P.op("t", lambda e: e.matmul(
                    psA[:, :], onesbf[:], kvbf[:, 0:512], start=True, stop=True),
                    deps=(t_kv[i], t_onesbf) + ((t_amm[i - 1],) if i > 0 else (t_asb0,)))
                ta0 = P.op("v", lambda e: e.tensor_add(
                    asb[0:1, 0:512], asb[0:1, 0:512], psA[:, :]), deps=(tm0,))
                tm1 = P.op("t", lambda e: e.matmul(
                    psA[:, :], onesbf[:], kvbf[:, 512:1024], start=True, stop=True),
                    deps=(ta0,))
                t_amm[i] = P.op("v", lambda e: e.tensor_add(
                    asb[0:1, 512:1024], asb[0:1, 512:1024], psA[:, :]), deps=(tm1,))

                if i + 2 < NT:
                    plan_stats_xn(i + 2)
                if i in (0, 2, 4):
                    plan_tabseg(i // 2 + 1)

            t_ast = P.op("s", lambda e: e.dma_start(
                out=a_in.ap()[:, :], in_=asb[0:1, :]), deps=(t_amm[NT - 1],), inc=16, sem="d_ai")
            t_acc = P.op("g", lambda e: e.collective_compute(
                "AllReduce", OP.add, replica_groups=groups,
                ins=[a_in.ap().opt()], outs=[a_out.ap().opt()]),
                deps=(t_ast,), inc=1, sem="cc_a")

            P.emit_phase(nc, sems)

        with ExitStack() as sR:
            er = sR.enter_context
            w1sb = [er(nc.sbuf_tensor(f"w1sb{c}", [128, FF], BF16)) for c in range(DC)]

            t_w1 = []

            with ExitStack() as s2:
                e2 = s2.enter_context
                xring2 = [e2(nc.sbuf_tensor(f"x2r{k}", [128, D], F32)) for k in range(3)]
                x2ring = [e2(nc.sbuf_tensor(f"x2o{k}", [128, D], F32)) for k in range(2)]
                h2f = [e2(nc.sbuf_tensor(f"h2f{k}", [128, D], F32)) for k in range(2)]
                h2bf = e2(nc.sbuf_tensor("h2bf", [128, D], BF16))
                basering = [e2(nc.sbuf_tensor(f"bas{k}", [128, D], F32)) for k in range(2)]
                ybf = [e2(nc.sbuf_tensor(f"ybf{k}", [128, D], F32R)) for k in range(2)]
                yT = [e2(nc.sbuf_tensor(f"yT{k}", [128, D], F32R)) for k in range(2)]
                woutT = e2(nc.sbuf_tensor("woutT", [128, DC * D], F32R))
                Abc = e2(nc.sbuf_tensor("Abc", [128, D], F32))
                wrbc = e2(nc.sbuf_tensor("wrbc", [128, D], F32))
                psq2 = [e2(nc.psum_tensor(f"psq2{k}", [128, D], F32)) for k in range(2)]
                psT2 = [e2(nc.psum_tensor(f"psT2{k}", [128, 512], F32R)) for k in range(2)]

                t_wout = []
                for c in range(DC):
                    t_wout.append(P.op("g", (lambda c: (lambda e: e.dma_start(
                        out=woutT[:, c * D:(c + 1) * D],
                        in_=woutT_h[128 * c:128 * (c + 1), :])))(c), inc=16, sem="d_wo"))
                for c in range(DC):
                    t_w1.append(P.op("g", (lambda c: (lambda e: e.dma_start(
                        out=w1sb[c][:], in_=w1_h[128 * c:(128 * (c + 1)), :])))(c), inc=16, sem="d_w1"))
                t_abc = P.op("s", lambda e: e.dma_start(
                    out=Abc[:], in_=a_out.ap().to_broadcast([128, D])),
                    deps=(t_acc,), inc=16, sem="d_abc")
                t_wrbc = P.op("s", lambda e: e.dma_start(
                    out=wrbc[:], in_=wr_h.to_broadcast([128, D])), inc=16, sem="d_wr")

                t_x2l = [None] * NT
                t_ybf = [None] * NT
                t_yT = [None] * NT
                t_wmm = [None] * NT
                t_x2 = [None] * NT
                t_st2 = [None] * NT
                t_h2v = [None] * NT
                t_h2st = [None] * NT
                t_base = [None] * NT
                t_bst = [None] * NT

                def plan_front(i):
                    depx = ()
                    if i >= 3:
                        depx = (t_x2[i - 3],)
                    t_x2l[i] = P.op("s", (lambda i: (lambda e: e.dma_start(
                        out=xring2[i % 3][:], in_=x_h[128 * i:128 * (i + 1), :])))(i),
                        deps=depx, inc=16, sem=f"d_x{i % 3 + 3}")

                    depy = [t_abc, t_identr]
                    if i >= 2:
                        depy.append(t_yT[i - 2])
                    t_ybf[i] = P.op("v", (lambda i: (lambda e: e.tensor_tensor(
                        out=ybf[i % 2][:], in0=l2nq[:, i * D:(i + 1) * D],
                        in1=Abc[:], op=OP.mult)))(i), deps=tuple(depy))

                    evs2 = []
                    tlast2 = None
                    for c in range(DC):
                        depT = [t_ybf[i]]
                        if c % 4 == 0 and i > 0:
                            depT.append(t_yT[i - 1])
                        tlast2 = P.op("t", (lambda i, c: (lambda e: e.transpose(
                            psT2[c // 4][:, (c % 4) * 128:(c % 4 + 1) * 128],
                            ybf[i % 2][:, c * 128:(c + 1) * 128],
                            identr[:])))(i, c), deps=tuple(depT))
                        if c == 3 or c == 7:
                            evs2.append(P.op("a", (lambda i, c: (lambda e: e.copy(
                                yT[i % 2][:, (c - 3) * 128:(c + 1) * 128],
                                psT2[c // 4][:])))(i, c), deps=(tlast2,)))
                    t_yT[i] = evs2[-1]

                    def wmm_fn(i):
                        def f(e):
                            inst = None
                            for c in range(DC):
                                for h in range(2):
                                    inst = e.matmul(
                                        psq2[i % 2][:, h * 512:(h + 1) * 512],
                                        yT[i % 2][:, c * 128:(c + 1) * 128],
                                        woutT[:, c * D + h * 512:c * D + (h + 1) * 512],
                                        start=(c == 0), stop=(c == DC - 1))
                            return inst
                        return f
                    depw = [t_yT[i], t_wout[-1]]
                    if i >= 2:
                        depw.append(t_x2[i - 2])
                    t_wmm[i] = P.op("t", wmm_fn(i), deps=tuple(depw))

                def plan_back_x2(i):
                    def x2_fn(i):
                        def f(e):
                            xr = xring2[i % 3]
                            e.tensor_scalar(out=dvejunk[:], in0=xr[:], scalar1=mu(i),
                                            scalar2=None, op0=OP.subtract)
                            e.scalar_tensor_tensor(out=dvejunk[:], in0=dvejunk[:],
                                                   scalar=rr(i), in1=xr[:],
                                                   op0=OP.mult, op1=OP.add)
                            return e.tensor_add(x2ring[i % 2][:], dvejunk[:],
                                                psq2[i % 2][:])
                        return f
                    depx2 = [t_wmm[i], t_x2l[i]]
                    if i >= 2:
                        depx2.extend((t_st2[i - 2], t_h2v[i - 2], t_base[i - 2]))
                    t_x2[i] = P.op("v", x2_fn(i), deps=tuple(depx2))

                    x2t = x2ring[i % 2]
                    tk = P.op("a", (lambda x2t: (lambda e: e.activation(
                        out=actjunk[:], in_=x2t[:], func=AF.Copy,
                        accum_out=scol[:, 0:1])))(x2t), deps=(t_x2[i],))
                    tk2 = P.op("a", (lambda x2t: (lambda e: e.activation(
                        out=actjunk[:], in_=x2t[:], func=AF.Square,
                        accum_out=scol[:, 1:2])))(x2t), deps=(tk,))
                    tk = P.op("a", (lambda i: (lambda e: e.mul(
                        mu2(i), scol[:, 0:1], 1.0 / D)))(i), deps=(tk,))
                    tk = P.op("a", (lambda i: (lambda e: e.activation(
                        out=tmps[:, 0:1], in_=mu2(i), func=AF.Square)))(i),
                        deps=(tk,))
                    tk = P.op("a", lambda e: e.activation(
                        out=tmps[:, 0:1], in_=tmps[:, 0:1], func=AF.Copy,
                        scale=-1.0, bias=EPS_LN), deps=(tk,))
                    tk = P.op("a", lambda e: e.activation(
                        out=tmps[:, 1:2], in_=scol[:, 1:2], func=AF.Ln,
                        scale=1.0 / D, bias=tmps[:, 0:1]), deps=(tk, tk2))
                    t_st2[i] = P.op("a", (lambda i: (lambda e: e.activation(
                        out=rr2(i), in_=tmps[:, 1:2], func=AF.Exp, scale=-0.5)))(i),
                        deps=(tk,))

                def plan_back_h2(i):
                    def h2_fn(i):
                        def f(e):
                            e.tensor_scalar(out=h2f[i % 2][:], in0=x2ring[i % 2][:],
                                            scalar1=mu2(i), scalar2=None,
                                            op0=OP.subtract)
                            e.tensor_scalar(out=h2f[i % 2][:], in0=h2f[i % 2][:],
                                            scalar1=rr2(i), scalar2=None, op0=OP.mult)
                            return e.scalar_tensor_tensor(
                                out=dvejunk[:], in0=h2f[i % 2][:], scalar=1.0,
                                in1=wrbc[:], op0=OP.mult, op1=OP.mult,
                                accum_out=scloc[:, i:i + 1])
                        return f
                    deph = [t_st2[i], t_wrbc]
                    if i >= 2:
                        deph.extend((t_h2st[i - 2], t_base[i - 2]))
                    t_h2v[i] = P.op("v", h2_fn(i), deps=tuple(deph))

                    depc = [t_h2v[i]]
                    if i > 0:
                        depc.append(t_h2st[i - 1])
                    t_h2c = P.op("v", (lambda i: (lambda e: e.tensor_copy(
                        out=h2bf[:], in_=h2f[i % 2][:])))(i), deps=tuple(depc))
                    t_h2st[i] = P.op("s", (lambda i: (lambda e: e.dma_start(
                        out=h2tok[128 * i:128 * (i + 1), :], in_=h2bf[:])))(i),
                        deps=(t_h2c,), inc=16, sem="d_h2")

                    depb = [t_h2v[i]]
                    if i >= 2:
                        depb.append(t_bst[i - 2])
                    t_base[i] = P.op("g", (lambda i: (lambda e: e.tensor_add(
                        basering[i % 2][:], h2f[i % 2][:], x2ring[i % 2][:])))(i),
                        deps=tuple(depb))
                    t_bst[i] = P.op("g", (lambda i: (lambda e: e.dma_start(
                        out=y_h[128 * i:128 * (i + 1), :], in_=basering[i % 2][:])))(i),
                        deps=(t_base[i],), inc=16, sem=f"d_b{i % 2}")

                plan_front(0)
                plan_front(1)
                for i in range(NT):
                    plan_back_x2(i)
                    if i + 2 < NT:
                        plan_front(i + 2)
                    if i >= 1:
                        plan_back_h2(i - 1)
                plan_back_h2(NT - 1)

                t_scst = P.op("s", lambda e: e.dma_start(
                    out=sc_in.ap()[:, :], in_=scloc[:]),
                    deps=tuple(t_h2v), inc=16, sem="d_sci")
                t_cc2 = P.op("g", lambda e: e.collective_compute(
                    "AllGather", OP.bypass, replica_groups=groups,
                    ins=[sc_in.ap().opt()], outs=[sc_out.ap().opt()]),
                    deps=(t_scst,), inc=1, sem="cc_sc")
                t_scg = P.op("s", lambda e: e.dma_start(
                    out=scg[:],
                    in_=sc_out.ap().rearrange("r (p j) -> (r p) j", p=32)),
                    deps=(t_cc2,), inc=16, sem="d_scg")

                P.emit_phase(nc, sems)

            with ExitStack() as s3:
                e3 = s3.enter_context
                psC = e3(nc.psum_tensor("psC", [128, 1], F32))
                psP = e3(nc.psum_tensor("psP", [128, 1], F32))

                LO = bis[:, 0:1]
                MID = bis[:, 1:2]
                SELM = bis[:, 2:3]
                CNT = bis[:, 3:4]

                prev = P.op("v", lambda e: e.memset(LO, -16384.0), deps=(t_scg,))
                for k in range(BIS):
                    hw = float(2.0 ** (14 - k))
                    t_mid = P.op("v", (lambda hw: (lambda e: e.tensor_scalar(
                        out=MID, in0=LO, scalar1=hw, scalar2=None, op0=OP.add)))(hw),
                        deps=(prev,))
                    t_cmp = P.op("v", lambda e: e.tensor_scalar(
                        out=bmask[:], in0=scg[:], scalar1=MID, scalar2=None,
                        op0=OP.is_ge, op1=OP.add, accum_out=ccol[:]),
                        deps=(t_mid,))
                    t_cm = P.op("t", lambda e: e.matmul(
                        psC[:, :], onesf[:], ccol[:], start=True, stop=True),
                        deps=(t_cmp, t_onesf))
                    t_sel = P.op("v", lambda e: e.tensor_tensor(
                        out=SELM, in0=psC[:], in1=consts[:, 3:4], op=OP.is_ge),
                        deps=(t_cm,))
                    prev = P.op("v", (lambda hw: (lambda e: e.scalar_tensor_tensor(
                        out=LO, in0=SELM, scalar=hw, in1=LO,
                        op0=OP.mult, op1=OP.add)))(hw), deps=(t_sel,))
                t_tau = prev

                t_m = P.op("v", lambda e: e.tensor_scalar(
                    out=mloc[:], in0=scloc[:], scalar1=LO, scalar2=None,
                    op0=OP.is_ge, op1=OP.add, accum_out=rowcnt[:]), deps=(t_tau,))
                t_pb = P.op("t", lambda e: e.matmul(
                    psP[:, :], l128[:], rowcnt[:], start=True, stop=True),
                    deps=(t_m, t_l128))
                t_pbe = P.op("v", lambda e: e.tensor_copy(out=pbase[:], in_=psP[:]),
                             deps=(t_pb,))

                tk = P.op("v", lambda e: e.tensor_add(
                    pfxa[:, 1:], mloc[:, 0:NT - 1], mloc[:, 1:]), deps=(t_m,))
                tk = P.op("v", lambda e: e.tensor_copy(
                    out=pfxa[:, 0:1], in_=mloc[:, 0:1]), deps=(tk,))
                tk = P.op("v", lambda e: e.tensor_add(
                    pfxb[:, 2:], pfxa[:, 0:NT - 2], pfxa[:, 2:]), deps=(tk,))
                tk = P.op("v", lambda e: e.tensor_copy(
                    out=pfxb[:, 0:2], in_=pfxa[:, 0:2]), deps=(tk,))
                tk = P.op("v", lambda e: e.tensor_add(
                    pfxa[:, 4:], pfxb[:, 0:NT - 4], pfxb[:, 4:]), deps=(tk,))
                tk = P.op("v", lambda e: e.tensor_copy(
                    out=pfxa[:, 0:4], in_=pfxb[:, 0:4]), deps=(tk,))
                tk = P.op("v", lambda e: e.tensor_add(
                    pfxb[:, 8:], pfxa[:, 0:NT - 8], pfxa[:, 8:]), deps=(tk,))
                tk = P.op("v", lambda e: e.tensor_copy(
                    out=pfxb[:, 0:8], in_=pfxa[:, 0:8]), deps=(tk,))
                t_pfx = P.op("v", lambda e: e.tensor_sub(
                    pfxa[:], pfxb[:], mloc[:]), deps=(tk,))

                tk = P.op("v", lambda e: e.tensor_scalar(
                    out=slot[:], in0=pfxa[:], scalar1=pbase[:], scalar2=None,
                    op0=OP.add), deps=(t_pfx, t_pbe))
                tk = P.op("v", lambda e: e.tensor_scalar(
                    out=slot[:], in0=slot[:], scalar1=BIGF, scalar2=None,
                    op0=OP.add), deps=(tk,))
                tk = P.op("v", lambda e: e.scalar_tensor_tensor(
                    out=slot[:], in0=mloc[:], scalar=-BIGF, in1=slot[:],
                    op0=OP.mult, op1=OP.add), deps=(tk,))
                tk = P.op("v", lambda e: e.tensor_copy(out=dbgA[:], in_=slot[:]),
                          deps=(tk,))
                tk = P.op("v", lambda e: e.tensor_scalar(
                    out=pfxb[:], in0=slot[:], scalar1=-63.5, scalar2=1.0 / 128.0,
                    op0=OP.add, op1=OP.mult), deps=(tk,))
                tk = P.op("v", lambda e: e.tensor_copy(out=sloti[:], in_=pfxb[:]),
                          deps=(tk,))
                tk = P.op("v", lambda e: e.tensor_copy(out=pfxb[:], in_=sloti[:]),
                          deps=(tk,))
                tk = P.op("v", lambda e: e.tensor_copy(out=dbgB[:], in_=pfxb[:]),
                          deps=(tk,))
                tk = P.op("v", lambda e: e.scalar_tensor_tensor(
                    out=slot[:], in0=pfxb[:], scalar=-128.0, in1=slot[:],
                    op0=OP.mult, op1=OP.add), deps=(tk,))
                t_rank = P.op("v", lambda e: e.scalar_tensor_tensor(
                    out=slot[:], in0=slot[:], scalar=float(NG), in1=pfxb[:],
                    op0=OP.mult, op1=OP.add), deps=(tk,))
                t_spi = P.op("v", lambda e: e.tensor_copy(out=sloti[:], in_=slot[:]),
                             deps=(t_rank,))

                tkp = P.op("v", lambda e: e.tensor_copy(
                    out=pairs[:, 0:2 * NT:2], in_=scloc[:]), deps=(t_rank, t_idloc))
                t_pairs = P.op("v", lambda e: e.tensor_copy(
                    out=pairs[:, 1:2 * NT:2], in_=idloc[:]), deps=(tkp,))

                t_swid = []
                for i in range(NT):
                    t_swid.append(P.op("g", (lambda i: (lambda e: e.indirect_dma_start(
                        out=widd[:, :],
                        out_offset=IndirectOffsetOnAxis(sloti[:, i:i + 1], axis=0),
                        in_=pairs[:, 2 * i:2 * i + 2], in_offset=None,
                        bounds_check=CAP - 1, oob_is_err=False)))(i),
                        deps=(t_spi, t_pairs, t_pre_dma), inc=16, sem="d_wid"))

                t_widld = P.op("s", lambda e: e.dma_start(
                    out=widpm[:], in_=widd.rearrange("(p j) two -> p (j two)", p=128)),
                    deps=tuple(t_swid), inc=16, sem="d_wid")
                t_wslot = P.op("a", lambda e: e.activation(
                    out=wslot[:], in_=widpm[:, 0:2 * NG:2], func=AF.Sigmoid),
                    deps=(t_widld,))
                t_ids = P.op("v", lambda e: e.tensor_copy(
                    out=idsl[:], in_=widpm[:, 1:2 * NG:2]), deps=(t_widld,))

                P.emit_phase(nc, sems)

            with ExitStack() as s4:
                e4 = s4.enter_context
                w2sb = [e4(nc.sbuf_tensor(f"w2sb{c}", [128, D], BF16))
                        for c in range(FC)]
                GT = [e4(nc.sbuf_tensor(f"GT{c}", [128, CAP], BF16))
                      for c in range(DC)]
                gtok = [e4(nc.sbuf_tensor(f"gtok{k}", [128, D], BF16))
                        for k in range(2)]
                hblk = [l2nq[:, 0:FC * 256], l2nq[:, FC * 256:2 * FC * 256]]
                b2bc = e4(nc.sbuf_tensor("b2bc", [128, D], F32))
                upd = [actjunk[:], dvejunk[:]]
                psT3 = [e4(nc.psum_tensor(f"psT3{k}", [128, 512], BF16))
                        for k in range(2)]
                psH = [e4(nc.psum_tensor(f"psH{k}", [128, 256], F32))
                       for k in range(2)]
                psU = [e4(nc.psum_tensor(f"psU{k}", [128, 512], F32))
                       for k in range(4)]

                t_b2bc = P.op("s", lambda e: e.dma_start(
                    out=b2bc[:], in_=b2_h.to_broadcast([128, D])), inc=16, sem="d_b2")
                t_w2 = []
                for c in range(FC):
                    t_w2.append(P.op("g", (lambda c: (lambda e: e.dma_start(
                        out=w2sb[c][:], in_=w2_h[128 * c:128 * (c + 1), :])))(c),
                        inc=16, sem="d_w2"))

                t_gz = [P.op("v", (lambda k: (lambda e: e.memset(
                    gtok[k][:], 0.0)))(k)) for k in range(2)]
                t_gath = [None] * NG
                t_gtr = [None] * NG
                for j in range(NG):
                    depg = [t_ids, t_gz[j % 2]] + list(t_h2st)
                    if j >= 2:
                        depg.append(t_gtr[j - 2])
                    t_gath[j] = P.op("g", (lambda j: (lambda e: e.indirect_dma_start(
                        out=gtok[j % 2][:], out_offset=None,
                        in_=h2tok[:, :],
                        in_offset=IndirectOffsetOnAxis(idsl[:, j:j + 1], axis=0),
                        bounds_check=T - 1, oob_is_err=False)))(j),
                        deps=tuple(depg), inc=16, sem=f"d_ga{j % 2}")
                    evs3 = []
                    tl3 = None
                    for c in range(DC):
                        depT = [t_gath[j]]
                        if c % 4 == 0 and j > 0:
                            depT.append(t_gtr[j - 1])
                        tl3 = P.op("t", (lambda j, c: (lambda e: e.transpose(
                            psT3[c // 4][:, (c % 4) * 128:(c % 4 + 1) * 128],
                            gtok[j % 2][:, c * 128:(c + 1) * 128],
                            ident[:])))(j, c), deps=tuple(depT))
                        if c == 3 or c == 7:
                            for cc in range(c - 3, c + 1):
                                evs3.append(P.op("a", (lambda j, cc: (lambda e: e.copy(
                                    GT[cc][:, j * 128:(j + 1) * 128],
                                    psT3[cc // 4][:, (cc % 4) * 128:(cc % 4 + 1) * 128])))(j, cc),
                                    deps=(tl3,)))
                    t_gtr[j] = evs3[-1]

                gelu_hist = []
                t_ffn2_last = {}
                t_uev = {}
                t_scat = []
                for b in range(NBLK):
                    for fc in range(FC):
                        def ffn1_fn(b, fc):
                            def f(e):
                                inst = None
                                for c in range(DC):
                                    inst = e.matmul(
                                        psH[fc % 2][:, :],
                                        w1sb[c][:, fc * 128:(fc + 1) * 128],
                                        GT[c][:, b * 256:(b + 1) * 256],
                                        start=(c == 0), stop=(c == DC - 1))
                                return inst
                            return f
                        dep1 = [t_gtr[2 * b], t_gtr[2 * b + 1], t_w1[-1]]
                        if len(gelu_hist) >= 2:
                            dep1.append(gelu_hist[-2])
                        tmm = P.op("t", ffn1_fn(b, fc), deps=tuple(dep1))
                        depg2 = [tmm, t_b1]
                        if b == 0 and fc == 0:
                            depg2.append(t_ybf[NT - 1])
                        if b >= 2 and fc == 0:
                            depg2.append(t_ffn2_last[b - 2])
                        gelu_hist.append(P.op("a", (lambda b, fc: (lambda e:
                            e.activation(
                                out=hblk[b % 2][:, fc * 256:(fc + 1) * 256],
                                in_=psH[fc % 2][:], func=AF.Gelu_apprx_tanh,
                                bias=b1pm[:, fc:fc + 1])))(b, fc),
                            deps=tuple(depg2)))

                    for fc in range(FC):
                        def ffn2_fn(b, fc):
                            def f(e):
                                inst = None
                                for tc in range(2):
                                    for h in range(2):
                                        inst = e.matmul(
                                            psU[tc * 2 + h][:, :],
                                            hblk[b % 2][:, fc * 256 + tc * 128:
                                                         fc * 256 + (tc + 1) * 128],
                                            w2sb[fc][:, h * 512:(h + 1) * 512],
                                            start=(fc == 0), stop=(fc == FC - 1))
                                return inst
                            return f
                        dep2 = [gelu_hist[b * FC + fc], t_w2[-1]]
                        if b > 0 and fc == 0:
                            dep2.extend(t_uev[b - 1])
                        t_ffn2_last[b] = P.op("t", ffn2_fn(b, fc), deps=tuple(dep2))

                    uevs = []
                    for tc in range(2):
                        def uev_fn(b, tc):
                            def f(e):
                                for h in range(2):
                                    e.tensor_add(upd[tc][:, h * 512:(h + 1) * 512],
                                                 psU[tc * 2 + h][:, :],
                                                 b2bc[:, h * 512:(h + 1) * 512])
                                return e.tensor_scalar(
                                    out=upd[tc], in0=upd[tc],
                                    scalar1=wslot[:, 2 * b + tc:2 * b + tc + 1],
                                    scalar2=None, op0=OP.mult)
                            return f
                        depu = [t_ffn2_last[b], t_b2bc, t_wslot]
                        if b == 0:
                            depu.append(t_st2[NT - 1])
                        if len(t_scat) >= 2:
                            depu.append(t_scat[-2])
                        ev = P.op("v", uev_fn(b, tc), deps=tuple(depu))
                        uevs.append(ev)
                        t_scat.append(P.op("g", (lambda b, tc: (lambda e:
                            e.indirect_dma_start(
                                out=y_h[:, :],
                                out_offset=IndirectOffsetOnAxis(
                                    idsl[:, 2 * b + tc:2 * b + tc + 1], axis=0),
                                in_=upd[tc], in_offset=None,
                                bounds_check=T - 1, oob_is_err=False,
                                compute_op=OP.add)))(b, tc),
                            deps=tuple([ev] + list(t_bst)), inc=16, sem=f"d_sc{tc}"))
                    t_uev[b] = uevs

                if debug:
                    dbg_deps = tuple(t_scat) + tuple(t_bst) + (t_tau, t_wslot, t_ids, t_acc)
                    for c in range(DC):
                        P.op("g", (lambda c: (lambda e: e.dma_start(
                            out=dbg_gt[c * 128:(c + 1) * 128, :], in_=GT[c][:])))(c),
                             deps=dbg_deps + tuple(t_gtr), inc=16)
                    P.op("g", lambda e: e.dma_start(out=dbg_stats[:, :], in_=stats[:]),
                         deps=dbg_deps, inc=16)
                    P.op("g", lambda e: e.dma_start(out=dbg_sc[:, :], in_=scloc[:]),
                         deps=dbg_deps, inc=16)
                    P.op("g", lambda e: e.dma_start(out=dbg_bis[:, :], in_=bis[:]),
                         deps=dbg_deps, inc=16)
                    P.op("g", lambda e: e.dma_start(out=dbg_wid[:, :], in_=widpm[:]),
                         deps=dbg_deps, inc=16)
                    P.op("g", lambda e: e.dma_start(out=dbg_aout[:, :], in_=a_out.ap()[:, :]),
                         deps=dbg_deps, inc=16)
                    P.op("g", lambda e: e.dma_start(out=dbg_h2[:, :], in_=h2tok[:, :]),
                         deps=dbg_deps, inc=16)
                    P.op("g", lambda e: e.dma_start(out=dbg_l2nq[:, :], in_=l2nq[:]),
                         deps=dbg_deps, inc=16)
                    P.op("g", lambda e: e.dma_start(out=dbg_mloc[:, :], in_=mloc[:]),
                         deps=dbg_deps, inc=16)
                    P.op("g", lambda e: e.dma_start(out=dbg_slot[:, :], in_=slot[:]),
                         deps=dbg_deps, inc=16)
                    P.op("g", lambda e: e.dma_start(out=dbg_sloti[:, :], in_=sloti[:]),
                         deps=dbg_deps, inc=16)
                    P.op("v", lambda e: e.tensor_copy(out=bis[:, 4:5], in_=rowcnt[:]),
                         deps=dbg_deps)
                    t_dbgpb = P.op("v", lambda e: e.tensor_copy(out=bis[:, 5:6], in_=pbase[:]),
                         deps=dbg_deps)
                    P.op("g", lambda e: e.dma_start(out=dbg_rc[:, :], in_=bis[:, 4:6]),
                         deps=dbg_deps + (t_dbgpb,), inc=16)
                    P.op("g", lambda e: e.dma_start(out=dbg_widd[:, :], in_=widd[:, :]),
                         deps=dbg_deps, inc=16)
                    P.op("g", lambda e: e.dma_start(out=dbg_A2[:, :], in_=dbgA[:]),
                         deps=dbg_deps, inc=16)
                    P.op("g", lambda e: e.dma_start(out=dbg_B2[:, :], in_=dbgB[:]),
                         deps=dbg_deps, inc=16)
                    P.op("g", lambda e: e.dma_start(out=dbg_C2[:, :], in_=dbgC[:]),
                         deps=dbg_deps, inc=16)
                    P.op("g", lambda e: e.dma_start(out=dbg_pairs[:, :], in_=pairs[:]),
                         deps=dbg_deps, inc=16)
                P.op("g", lambda e: e.engine_nop(),
                     deps=tuple(t_scat) + tuple(t_bst))

                P.emit_phase(nc, sems)

    return nc


_NC_CACHE = {}


def _get_nc():
    if "nc" not in _NC_CACHE:
        _NC_CACHE["nc"] = build_nc()
    return _NC_CACHE["nc"]


def kernel(L0_attn_X, L0_attn_W_in, L0_attn_W_out, L0_attn_ln_g, L0_attn_ln_b,
           L0_ffn_w_router, L0_ffn_W1, L0_ffn_b1, L0_ffn_W2, L0_ffn_b2):
    X = np.ascontiguousarray(np.asarray(L0_attn_X, dtype=np.float32)
                             .reshape(2 * 8192, D))
    w_inT = np.ascontiguousarray(np.asarray(L0_attn_W_in, np.float32).T)
    w_outT = np.ascontiguousarray(np.asarray(L0_attn_W_out, np.float32).T)
    w1 = np.ascontiguousarray(np.asarray(L0_ffn_W1, np.float32))
    w2 = np.ascontiguousarray(np.asarray(L0_ffn_W2, np.float32))
    g_vec = np.ascontiguousarray(np.asarray(L0_attn_ln_g, np.float32)
                                 .reshape(1, D))
    b_vec = np.ascontiguousarray(np.asarray(L0_attn_ln_b, np.float32)
                                 .reshape(1, D))
    wr_vec = np.ascontiguousarray(np.asarray(L0_ffn_w_router, np.float32)
                                  .reshape(1, D))
    b2_vec = np.ascontiguousarray(np.asarray(L0_ffn_b2, np.float32).reshape(1, D))
    b1_pm = np.ascontiguousarray(np.asarray(L0_ffn_b1, np.float32)
                                 .reshape(FC, 128).T)
    identity_f32 = np.eye(128, dtype=np.float32)
    import ml_dtypes
    identity = identity_f32.astype(ml_dtypes.bfloat16)
    l128v = np.triu(np.ones((128, 128), np.float32), k=1)

    in_maps = []
    for c in range(NCORES):
        pos0 = (c % 4) * T
        post = (pos0 + 128 * np.arange(NT)[None, :] +
                np.arange(128)[:, None]).astype(np.float32)
        in_maps.append({
            "x": X[c * T:(c + 1) * T],
            "w_inT": w_inT, "w_outT": w_outT, "w1": w1, "w2": w2,
            "g_vec": g_vec, "b_vec": b_vec, "wr_vec": wr_vec, "b2_vec": b2_vec,
            "b1_pm": b1_pm, "postile": post,
            "identity": identity, "identity_f": identity_f32, "l128": l128v,
            "ones_r": np.ones((128, 1), np.float32),
        })

    nc = _get_nc()
    res = run_bass_kernel_spmd(nc, in_maps, core_ids=list(range(NCORES)))
    out = np.concatenate([res.results[c]["y"] for c in range(NCORES)], axis=0)
    return out.reshape(2, 8192, D).astype(np.float32)
